# version 01
# baseline (speedup 1.0000x reference)
import numpy as np
import ml_dtypes
import concourse.bass as bass
import concourse.mybir as mybir
from concourse.bass_utils import run_bass_kernel_spmd
from contextlib import ExitStack

F32 = mybir.dt.float32
BF16 = mybir.dt.bfloat16
AF = mybir.ActivationFunctionType
ALU = mybir.AluOpType
AX = mybir.AxisListType

D = 4096
T = 1024
NC32 = 32
EPS = 1e-5
RS128 = float(1.0 / np.sqrt(128.0))
CSHIFT = 10.0
ALIBI_SLOPES = [2.0 ** (-(i + 1) * 0.25) for i in range(32)]

ENGS = ("sync", "scalar", "vector", "tensor", "gpsimd")

DEBUG_DUMPS = []
PHASES = ("ln", "attnprep", "softmax", "xa", "ffn")
N_LAYERS = 2


class Prog:
    def __init__(self, nc):
        self.nc = nc
        self.q = {e: [] for e in ENGS}
        self.cnt = {}
        self.sem = {}
        self.waited = {e: {} for e in ENGS}

    def new_sem(self, stack, name):
        self.sem[name] = stack.enter_context(self.nc.semaphore(name))
        self.cnt[name] = 0

    def wait(self, eng, sem, val=None):
        if val is None:
            val = self.cnt[sem]
        if val <= 0 or self.waited[eng].get(sem, 0) >= val:
            return
        self.waited[eng][sem] = val
        h = self.sem[sem]
        self.q[eng].append(lambda e, h=h, v=val: e.wait_ge(h, v))

    def run(self, eng, fn, inc=None, n=1):
        if inc is None:
            self.q[eng].append(fn)
        else:
            self.cnt[inc] += n
            h = self.sem[inc]
            self.q[eng].append(lambda e, f=fn, h=h, n=n: f(e).then_inc(h, n))

    def join(self):
        snapshot = dict(self.cnt)
        for e in ENGS:
            for sname, v in snapshot.items():
                self.wait(e, sname, v)

    def emit(self, block):
        for e in ENGS:
            getattr(block, e)(lambda eng, q=self.q[e]: [f(eng) for f in q])

    def init_j(self, eng, etype):
        self._jbox = getattr(self, "_jbox", {})
        box = {}
        self._jbox[eng] = box
        nc = self.nc

        def go(e):
            pid = nc.partition_id(engines=[etype])
            box["j"] = e.snap(pid % 4)
        self.q[eng].append(go)

    def run_switch(self, eng, body_fns, incs):
        for s_, n_ in incs:
            self.cnt[s_] += n_
        box = self._jbox[eng]

        def go(e):
            for i in e.Switch(box["j"], 4):
                body_fns[i](e)
        self.q[eng].append(go)



def emit_consts(p, t):
    r = lambda fn: p.run("gpsimd", fn, inc="pool")
    r(lambda e: e.memset(t["ones_bf"][:], 1.0))
    r(lambda e: e.memset(t["ones1_f"][:], 1.0))
    r(lambda e: e.memset(t["ident_bf"][:], 0.0))
    r(lambda e: e.affine_select(t["ident_bf"][:], t["ident_bf"][:],
                                compare_op=ALU.not_equal, fill=1.0, base=0,
                                pattern=[[-1, 128]], channel_multiplier=1))
    r(lambda e: e.iota(t["iota_vec"][:], pattern=[[1, 1]], base=0,
                       channel_multiplier=1, allow_small_or_imprecise_dtypes=True))
    r(lambda e: e.memset(t["cval"][:, 0:1], EPS))
    r(lambda e: e.memset(t["cval"][:, 1:2], -0.5))
    r(lambda e: e.memset(t["ones8_f"][:], 1.0))
    r(lambda e: e.memset(t["ones1_bf"][:], 1.0))
    r(lambda e: e.iota(t["iota_f"][:], pattern=[[1, 512]], base=0,
                       channel_multiplier=0, allow_small_or_imprecise_dtypes=True))


def emit_l0_consts(p, t):
    r = lambda fn: p.run("gpsimd", fn, inc="pool")
    r(lambda e: e.memset(t["ident_f"][:], 0.0))
    r(lambda e: e.affine_select(t["ident_f"][:], t["ident_f"][:],
                                compare_op=ALU.not_equal, fill=1.0, base=0,
                                pattern=[[-1, 128]], channel_multiplier=1))
    for k in range(32):
        sl = t["sel"][:, k * 128:(k + 1) * 128]
        r(lambda e, sl=sl: e.memset(sl, 1.0))
        r(lambda e, sl=sl, k=k: e.affine_select(
            sl, sl, compare_op=ALU.is_equal, fill=0.0, base=-4 * k,
            pattern=[[-1, 4], [0, 32]], channel_multiplier=1))


def emit_m01(p, t):
    p.run("gpsimd", lambda e: e.iota(
        t["iota_bv"][:], pattern=[[128, 36]], base=-4096, channel_multiplier=1,
        allow_small_or_imprecise_dtypes=True), inc="pool")
    r = lambda fn: p.run("gpsimd", fn, inc="pool")
    for i in range(4):
        sl = t["m01"][:, i * 512:(i + 1) * 512]
        r(lambda e, sl=sl: e.memset(sl, 0.0))
        r(lambda e, sl=sl, i=i: e.affine_select(
            sl, sl, compare_op=ALU.is_ge, fill=-10000.0, base=-i * 128,
            pattern=[[1, 512]], channel_multiplier=-1))


def emit_ln(p, t, x_dram, out_res):
    pst, psa, psb = t["ps_st"], t["ps_a"], t["ps_b"]
    xbb, sqb, xc = t["xb_bf"], t["sq_bf"], t["xc_f"]
    mu, var, a_ln, b_ln = t["r_mu"], t["r_var"], t["r_a"], t["r_b"]
    p.wait("tensor", "pool")
    consumed = {}
    for k in range(NC32):
        buf = xbb[:, (k % 2) * 1024:(k % 2) * 1024 + 1024]
        if k >= 2:
            p.wait("gpsimd", "pe", consumed[k - 2])
        p.run("gpsimd", lambda e, buf=buf, k=k: e.dma_start(
            out=buf, in_=x_dram[k * 128:(k + 1) * 128, :]), inc="wld", n=16)
        p.wait("scalar", "wld")
        sq = sqb[:, (k % 2) * 1024:(k % 2) * 1024 + 1024]
        p.run("scalar", lambda e, buf=buf, sq=sq: e.activation(sq, buf, AF.Square),
              inc="act")
        p.wait("tensor", "act")
        for th in range(2):
            p.run("tensor", lambda e, buf=buf, th=th, k=k: e.matmul(
                pst[:, th * 512:th * 512 + 512], t["ones_bf"][:],
                buf[:, th * 512:th * 512 + 512],
                start=(k == 0), stop=(k == NC32 - 1)))
        for th in range(2):
            p.run("tensor", lambda e, sq=sq, th=th, k=k: e.matmul(
                pst[:, 1024 + th * 512:1024 + th * 512 + 512], t["ones_bf"][:],
                sq[:, th * 512:th * 512 + 512],
                start=(k == 0), stop=(k == NC32 - 1)),
                inc=("pe" if th == 1 else None))
        consumed[k] = p.cnt["pe"]
    p.wait("vector", "pe")
    p.run("vector", lambda e: e.tensor_scalar_mul(mu[:], pst[0:1, 0:1024], 1.0 / D))
    p.run("vector", lambda e: e.tensor_scalar_mul(var[:], pst[0:1, 1024:2048], 1.0 / D))
    p.run("vector", lambda e: e.tensor_mul(a_ln[:], mu[:], mu[:]))
    p.run("vector", lambda e: e.tensor_sub(var[:], var[:], a_ln[:]), inc="dve")
    p.wait("scalar", "dve")
    p.run("scalar", lambda e: e.activation(b_ln[:], var[:], AF.Ln,
                                           bias=t["cval"][0:1, 0:1]))
    p.run("scalar", lambda e: e.activation(a_ln[:], b_ln[:], AF.Exp,
                                           scale=t["cval"][0:1, 1:2]), inc="act")
    p.wait("vector", "act")
    p.run("vector", lambda e: e.tensor_mul(b_ln[:], mu[:], a_ln[:]))
    p.run("vector", lambda e: e.tensor_scalar_mul(b_ln[:], b_ln[:], -1.0), inc="dve")
    p.wait("tensor", "dve")
    for th in range(2):
        p.run("tensor", lambda e, th=th: e.matmul(
            psa[:, th * 512:th * 512 + 512], t["ones1_f"][:],
            a_ln[0:1, th * 512:th * 512 + 512], start=True, stop=True))
    for th in range(2):
        p.run("tensor", lambda e, th=th: e.matmul(
            psb[:, th * 512:th * 512 + 512], t["ones1_f"][:],
            b_ln[0:1, th * 512:th * 512 + 512], start=True, stop=True),
            inc=("pe" if th == 1 else None))
    done = {}
    for k in range(NC32):
        buf = xc[:, (k % 2) * 1024:(k % 2) * 1024 + 1024]
        if k >= 2:
            p.wait("sync", "dve", done[k - 2])
        p.run("sync", lambda e, buf=buf, k=k: e.dma_start(
            out=buf, in_=x_dram[k * 128:(k + 1) * 128, :]), inc="ld", n=16)
        p.wait("vector", "ld")
        p.wait("vector", "pe")
        dst = out_res[:, k * 1024:(k + 1) * 1024]
        p.run("vector", lambda e, buf=buf: e.tensor_mul(t["t1_f"][:], buf, psa[:]))
        p.run("vector", lambda e, dst=dst: e.tensor_add(dst, t["t1_f"][:], psb[:]),
              inc="dve")
        done[k] = p.cnt["dve"]


def emit_l0_attnprep(p, t, din):
    res = t["res_a"]
    consumed = {}
    for k in range(NC32):
        buf = t["w_db"][:, (k % 2) * 4096:(k % 2) * 4096 + 4096]
        if k >= 2:
            p.wait("gpsimd", "dve", consumed[k - 2])
        p.run("gpsimd", lambda e, buf=buf, k=k: e.dma_start(
            out=buf, in_=din["wq_r"][:, k * 4096:(k + 1) * 4096]), inc="wld", n=16)
        p.wait("vector", "wld")
        p.run("vector", lambda e, buf=buf: e.reduce_sum(
            t["wq_red"][:], buf.rearrange("p (d h) -> p d h", h=32), axis=AX.X))
        p.run("vector", lambda e, k=k: e.tensor_copy(
            t["wqsum"][:, k * 128:(k + 1) * 128], t["wq_red"][:]), inc="dve")
        consumed[k] = p.cnt["dve"]
    p.run("sync", lambda e: e.dma_start(out=t["bq32"][:], in_=din["bq"]), inc="ld", n=16)
    p.run("sync", lambda e: e.dma_start(out=t["bkv"][:, 0:1], in_=din["bk"]), inc="ld", n=16)
    p.run("sync", lambda e: e.dma_start(out=t["bkv"][:, 1:2], in_=din["bv"]), inc="ld", n=16)
    p.wait("vector", "ld")
    p.run("vector", lambda e: e.reduce_sum(t["bkv"][:, 2:3], t["bq32"][:], axis=AX.X),
          inc="dve")
    p.run("gpsimd", lambda e: e.dma_start(out=t["wk_sb"][:], in_=din["wk_r"]),
          inc="wld", n=16)
    p.run("gpsimd", lambda e: e.dma_start(out=t["wv_sb"][:], in_=din["wv_r"]),
          inc="wld", n=16)
    psq, psk, psv = t["ps_q"], t["ps_k"], t["ps_v"]
    p.wait("tensor", "dve")
    p.wait("tensor", "wld")
    for k in range(NC32):
        for th in range(2):
            mv = res[:, k * 1024 + th * 512:k * 1024 + th * 512 + 512]
            last = (k == NC32 - 1)
            p.run("tensor", lambda e, k=k, th=th, mv=mv: e.matmul(
                psq[:, th * 512:th * 512 + 512],
                t["wqsum"][:, k * 128:(k + 1) * 128], mv,
                start=(k == 0), stop=(k == NC32 - 1)))
            p.run("tensor", lambda e, k=k, th=th, mv=mv: e.matmul(
                psk[:, th * 512:th * 512 + 512],
                t["wk_sb"][:, k * 128:(k + 1) * 128], mv,
                start=(k == 0), stop=(k == NC32 - 1)))
            p.run("tensor", lambda e, k=k, th=th, mv=mv: e.matmul(
                psv[:, th * 512:th * 512 + 512],
                t["wv_sb"][:, k * 128:(k + 1) * 128], mv,
                start=(k == 0), stop=(k == NC32 - 1)),
                inc=("pe" if last and th == 1 else None))
    p.wait("vector", "pe")
    p.run("vector", lambda e: e.tensor_scalar(t["qsT"][:], psq[:], t["bkv"][:, 2:3],
                                              RS128, ALU.add, ALU.mult))
    p.run("vector", lambda e: e.tensor_scalar(t["kT"][:], psk[:], t["bkv"][:, 0:1],
                                              None, ALU.add))
    p.run("vector", lambda e: e.tensor_scalar(t["vT"][:], psv[:], t["bkv"][:, 1:2],
                                              None, ALU.add), inc="dve")
    p.wait("tensor", "dve")
    for src, dst in [(t["qsT"], t["qs_row"]), (t["kT"], t["k_row"]),
                     (t["vT"], t["v_row"])]:
        for tt in range(8):
            p.run("tensor", lambda e, src=src, tt=tt: e.transpose(
                t["ps_tr"][:], src[:, tt * 128:(tt + 1) * 128], t["ident_f"][:]),
                inc="pe")
            p.wait("vector", "pe")
            p.run("vector", lambda e, dst=dst, tt=tt: e.tensor_copy(
                dst[:, tt * 128:(tt + 1) * 128], t["ps_tr"][:]), inc="dve")
            p.wait("tensor", "dve")


def emit_l0_softmax(p, t):
    expb, dump = t["exp_db"], t["dump_f"]
    dve_done = {}
    for tt in range(8):
        krow = t["k_row"][:, tt * 128:(tt + 1) * 128]
        vrow = t["v_row"][:, tt * 128:(tt + 1) * 128]
        p.wait("scalar", "dve")
        for d in range(128):
            buf = expb[:, (d % 2) * 128:(d % 2) * 128 + 128]
            if d >= 2:
                p.wait("scalar", "dve", dve_done[d - 2])
            p.run("scalar", lambda e, buf=buf, krow=krow, tt=tt, d=d: e.activation(
                buf, krow, AF.Exp,
                scale=t["qs_row"][:, tt * 128 + d:tt * 128 + d + 1],
                accum_out=t["zrow"][:, d:d + 1]), inc="act")
            p.wait("vector", "act")
            p.run("vector", lambda e, buf=buf, vrow=vrow, d=d: e.scalar_tensor_tensor(
                dump[:], buf, 1.0, vrow, ALU.mult, ALU.mult,
                accum_out=t["numrow"][:, d:d + 1]), inc="dve")
            dve_done[d] = p.cnt["dve"]
        p.wait("vector", "act")
        p.run("vector", lambda e: e.reciprocal(t["rz"][:], t["zrow"][:]))
        p.run("vector", lambda e: e.tensor_mul(t["orow"][:], t["numrow"][:], t["rz"][:]),
              inc="dve")
        p.wait("tensor", "dve")
        p.run("tensor", lambda e: e.transpose(t["ps_tr"][:], t["orow"][:],
                                              t["ident_f"][:]), inc="pe")
        p.wait("vector", "pe")
        p.run("vector", lambda e, tt=tt: e.tensor_copy(
            t["oT_bf"][:, tt * 128:(tt + 1) * 128], t["ps_tr"][:]), inc="dve")
        p.wait("tensor", "dve")


def emit_l0_xa(p, t, din):
    res = t["res_a"]
    p.run("gpsimd", lambda e: e.dma_start(out=t["aT_sb"][:], in_=din["aT_r"]),
          inc="wld", n=16)
    p.run("sync", lambda e: e.dma_start(out=t["crep"][:], in_=din["coeffs_rep"]),
          inc="ld", n=16)
    p.run("sync", lambda e: e.dma_start(out=t["srep"][:], in_=din["scales_rep"]),
          inc="ld", n=16)
    p.wait("gpsimd", "ld")
    p.run("gpsimd", lambda e: e.tensor_mul(t["wer"][:], t["crep"][:], t["srep"][:]),
          inc="pool")
    psxa = t["ps_xa"]
    p.wait("tensor", "wld")
    for e2 in range(2):
        for k in range(NC32):
            for th in range(2):
                last = (e2 == 1 and k == NC32 - 1 and th == 1)
                p.run("tensor", lambda e, e2=e2, k=k, th=th: e.matmul(
                    psxa[:, e2 * 1024 + th * 512:e2 * 1024 + th * 512 + 512],
                    t["aT_sb"][:, k * 256 + e2 * 128:k * 256 + e2 * 128 + 128],
                    res[:, k * 1024 + th * 512:k * 1024 + th * 512 + 512],
                    start=(k == 0), stop=(k == NC32 - 1)),
                    inc=("pe" if last else None))
    p.wait("vector", "pe")
    p.wait("vector", "pool")
    for e2 in range(2):
        p.run("vector", lambda e, e2=e2: e.tensor_scalar(
            t["xaT_bf"][:, e2 * 1024:(e2 + 1) * 1024],
            psxa[:, e2 * 1024:(e2 + 1) * 1024],
            t["wer"][:, e2:e2 + 1], None, ALU.mult), inc="dve")


def emit_l0_ffn_update(p, t, din, x_dram, x1_dram):
    res = t["res_a"]
    p.run("gpsimd", lambda e: e.dma_start(out=t["bfT_sb"][:], in_=din["bfT_r"]),
          inc="wld", n=16)
    p.run("sync", lambda e: e.dma_start(out=t["l0bias"][:], in_=din["l0bias_c"]),
          inc="ld", n=16)
    psg, psob = t["ps_g"], t["ps_ob"]
    wconsumed = {}
    xconsumed = {}
    stconsumed = {}
    for oc in range(NC32):
        buf = t["w_db"][:, (oc % 2) * 4096:(oc % 2) * 4096 + 4096]
        if oc >= 2:
            p.wait("gpsimd", "pe", wconsumed[oc - 2])
        p.run("gpsimd", lambda e, buf=buf, oc=oc: e.dma_start(
            out=buf, in_=din["l0w_r"][oc]), inc="wld", n=16)
        xb = t["xc_f"][:, (oc % 2) * 1024:(oc % 2) * 1024 + 1024]
        if oc >= 2:
            p.wait("sync", "dve", xconsumed[oc - 2])
        p.run("sync", lambda e, xb=xb, oc=oc: e.dma_start(
            out=xb, in_=x_dram[oc * 128:(oc + 1) * 128, :]), inc="ld", n=16)
        p.wait("tensor", "wld")
        p.wait("tensor", "dve")
        p.wait("tensor", "act")
        for th in range(2):
            for k in range(NC32):
                p.run("tensor", lambda e, buf=buf, k=k, th=th: e.matmul(
                    psg[:, th * 512:th * 512 + 512], buf[:, k * 128:(k + 1) * 128],
                    res[:, k * 1024 + th * 512:k * 1024 + th * 512 + 512],
                    start=(k == 0), stop=False))
            for e2 in range(2):
                p.run("tensor", lambda e, oc=oc, e2=e2, th=th: e.matmul(
                    psg[:, th * 512:th * 512 + 512],
                    t["bfT_sb"][:, oc * 256 + e2 * 128:oc * 256 + e2 * 128 + 128],
                    t["xaT_bf"][:, e2 * 1024 + th * 512:e2 * 1024 + th * 512 + 512],
                    start=False, stop=(e2 == 1)))
        p.run("tensor", lambda e, oc=oc: e.matmul(
            psob[:, 0:512], t["sel"][:, oc * 128:(oc + 1) * 128],
            t["oT_bf"][:, 0:512], start=True, stop=True), inc="pe")
        p.run("tensor", lambda e, oc=oc: e.matmul(
            psob[:, 512:1024], t["sel"][:, oc * 128:(oc + 1) * 128],
            t["oT_bf"][:, 512:1024], start=True, stop=True), inc="pe")
        wconsumed[oc] = p.cnt["pe"]
        p.wait("scalar", "pe")
        p.wait("scalar", "ld")
        p.run("scalar", lambda e, oc=oc: e.activation(
            t["g_sb"][:], psg[:], AF.Silu, bias=t["l0bias"][:, oc:oc + 1]), inc="act")
        p.wait("vector", "act")
        p.wait("vector", "pe")
        p.wait("vector", "ld")
        p.wait("vector", "st")
        if "gall" in t:
            p.wait("sync", "act")
            p.wait("sync", "pe")
            p.run("sync", lambda e, oc=oc: e.dma_start(
                out=t["gall"][oc * 128:(oc + 1) * 128, :], in_=t["g_sb"][:]),
                inc="st", n=16)
            p.run("vector", lambda e: e.tensor_copy(t["t2_f"][:], psob[:]), inc="dve")
            p.wait("sync", "dve")
            p.run("sync", lambda e, oc=oc: e.dma_start(
                out=t["oball"][oc * 128:(oc + 1) * 128, :], in_=t["t2_f"][:]),
                inc="st", n=16)
            p.wait("vector", "st")
        p.run("vector", lambda e: e.tensor_mul(t["t1_f"][:], t["g_sb"][:], psob[:]))
        if "uall" in t:
            p.run("vector", lambda e: e.tensor_copy(t["t1_f"][:], t["t1_f"][:]), inc="dve")
            p.wait("sync", "dve")
            p.run("sync", lambda e, oc=oc: e.dma_start(
                out=t["uall"][oc * 128:(oc + 1) * 128, :], in_=t["t1_f"][:]),
                inc="st", n=16)
            p.wait("vector", "st")
        p.run("vector", lambda e, xb=xb: e.tensor_add(t["t2_f"][:], t["t1_f"][:], xb),
              inc="dve")
        xconsumed[oc] = p.cnt["dve"]
        p.wait("sync", "dve")
        p.run("sync", lambda e, oc=oc: e.dma_start(
            out=x1_dram[oc * 128:(oc + 1) * 128, :], in_=t["t2_f"][:]), inc="st", n=16)
        stconsumed[oc] = p.cnt["st"]





def emit_router_xs(p, t, din, xs_dram):
    res = t["res_a"]
    p.run("gpsimd", lambda e: e.dma_start(out=t["gate_sb"][:], in_=din["gate_r"]),
          inc="wld", n=16)
    p.run("gpsimd", lambda e: e.dma_start(out=t["ics_sb"][:], in_=din["ics"]),
          inc="wld", n=16)
    psr, pszb8, psis = t["ps_r8"], t["ps_zb8"], t["ps_is"]
    p.wait("tensor", "wld")
    p.wait("tensor", "pool")
    for k in range(NC32):
        for th in range(2):
            p.run("tensor", lambda e, k=k, th=th: e.matmul(
                psr[:, th * 512:th * 512 + 512],
                t["gate_sb"][:, k * 8:(k + 1) * 8],
                res[:, k * 1024 + th * 512:k * 1024 + th * 512 + 512],
                start=(k == 0), stop=(k == NC32 - 1)),
                inc=("pe" if k == NC32 - 1 and th == 1 else None))
    p.wait("scalar", "pe")
    p.run("scalar", lambda e: e.activation(t["expr8"][:], psr[:], AF.Exp), inc="act")
    p.wait("tensor", "act")
    for th in range(2):
        p.run("tensor", lambda e, th=th: e.matmul(
            t["ps_z8"][:, th * 512:th * 512 + 512], t["ones8_f"][:],
            t["expr8"][:, th * 512:th * 512 + 512], start=True, stop=True),
            inc=("pe" if th == 1 else None))
    p.wait("vector", "pe")
    p.run("vector", lambda e: e.reciprocal(t["rz8"][:], t["ps_z8"][0:1, :]), inc="dve")
    p.wait("tensor", "dve")
    for th in range(2):
        p.run("tensor", lambda e, th=th: e.matmul(
            pszb8[:, th * 512:th * 512 + 512], t["ones1_f"][0:1, 0:8],
            t["rz8"][0:1, th * 512:th * 512 + 512], start=True, stop=True),
            inc=("pe" if th == 1 else None))
    p.wait("vector", "pe")
    p.run("vector", lambda e: e.tensor_mul(t["routT"][:], t["expr8"][:], pszb8[:]),
          inc="dve")
    p.wait("tensor", "dve")
    done = {}
    for k in range(NC32):
        for th in range(2):
            p.run("tensor", lambda e, k=k, th=th: e.matmul(
                psis[:, th * 512:th * 512 + 512],
                t["ics_sb"][:, k * 128:(k + 1) * 128],
                t["routT"][:, th * 512:th * 512 + 512], start=True, stop=True),
                inc=("pe" if th == 1 else None))
        p.wait("vector", "pe")
        if k >= 1:
            p.wait("vector", "st", done[k - 1])
        p.run("vector", lambda e, k=k: e.tensor_mul(
            t["xs_bf"][:], res[:, k * 1024:(k + 1) * 1024], psis[:]), inc="dve")
        p.wait("sync", "dve")
        p.run("sync", lambda e, k=k: e.dma_start(
            out=xs_dram[k * 128:(k + 1) * 128, :], in_=t["xs_bf"][:]), inc="st", n=16)
        done[k] = p.cnt["st"]
        p.wait("tensor", "dve")


def emit_l1_qkv(p, t, din, contrib):
    res = t["res_a"]
    psqk, psst = t["ps_qk"], t["ps_st4"]
    wcons = {}
    dcons = {}
    for oc in range(96):
        buf = t["w_db"][:, (oc % 2) * 4096:(oc % 2) * 4096 + 4096]
        if oc >= 2:
            p.wait("gpsimd", "pe", wcons[oc - 2])
        p.run("gpsimd", lambda e, buf=buf, oc=oc: e.dma_start(
            out=buf, in_=din["wqkv_r"][oc]), inc="wld", n=16)
        p.wait("tensor", "wld")
        p.wait("tensor", "dve")
        for th in range(2):
            for k in range(NC32):
                last = (th == 1 and k == NC32 - 1)
                p.run("tensor", lambda e, buf=buf, k=k, th=th: e.matmul(
                    psqk[:, th * 512:th * 512 + 512], buf[:, k * 128:(k + 1) * 128],
                    res[:, k * 1024 + th * 512:k * 1024 + th * 512 + 512],
                    start=(k == 0), stop=(k == NC32 - 1)),
                    inc=("pe" if last else None))
        wcons[oc] = p.cnt["pe"]
        p.wait("vector", "pe")
        if oc >= 2:
            p.wait("vector", "st", dcons[oc - 2])
        qk = t["qk_db"][:, (oc % 2) * 1024:(oc % 2) * 1024 + 1024]
        p.run("vector", lambda e, qk=qk: e.tensor_scalar(
            qk, psqk[:], -8.0, 8.0, ALU.max, ALU.min), inc="dve")
        if oc < 64:
            p.wait("scalar", "dve")
            sq = t["sq_bf"][:, (oc % 2) * 1024:(oc % 2) * 1024 + 1024]
            p.run("scalar", lambda e, qk=qk, sq=sq: e.activation(sq, qk, AF.Square),
                  inc="act")
            p.wait("tensor", "act")
            st0 = (oc % 32 == 0)
            sp0 = (oc % 32 == 31)
            for th in range(2):
                p.run("tensor", lambda e, qk=qk, th=th, st0=st0, sp0=sp0: e.matmul(
                    psst[:, th * 512:th * 512 + 512], t["ones_bf"][:],
                    qk[:, th * 512:th * 512 + 512], start=st0, stop=sp0))
            for th in range(2):
                p.run("tensor", lambda e, sq=sq, th=th, st0=st0, sp0=sp0: e.matmul(
                    psst[:, 1024 + th * 512:1024 + th * 512 + 512], t["ones_bf"][:],
                    sq[:, th * 512:th * 512 + 512], start=st0, stop=sp0),
                    inc=("pe" if th == 1 else None))
            if sp0:
                base = 0 if oc < 32 else 2
                mu, var = t["r_mu"], t["r_var"]
                p.wait("vector", "pe")
                p.run("vector", lambda e: e.tensor_scalar_mul(mu[:], psst[0:1, 0:1024], 1.0 / D))
                p.run("vector", lambda e: e.tensor_scalar_mul(var[:], psst[0:1, 1024:2048], 1.0 / D))
                p.run("vector", lambda e: e.tensor_mul(t["r_a"][:], mu[:], mu[:]))
                p.run("vector", lambda e: e.tensor_sub(var[:], var[:], t["r_a"][:]), inc="dve")
                p.wait("scalar", "dve")
                p.run("scalar", lambda e: e.activation(t["r_b"][:], var[:], AF.Ln,
                                                       bias=t["cval"][0:1, 0:1]))
                p.run("scalar", lambda e, base=base: e.activation(
                    t["ab4"][0:1, base * 1024:base * 1024 + 1024], t["r_b"][:],
                    AF.Exp, scale=t["cval"][0:1, 1:2]), inc="act")
                p.wait("vector", "act")
                p.run("vector", lambda e, base=base: e.tensor_mul(
                    t["r_b"][:], mu[:], t["ab4"][0:1, base * 1024:base * 1024 + 1024]))
                p.run("vector", lambda e, base=base: e.tensor_scalar_mul(
                    t["ab4"][0:1, (base + 1) * 1024:(base + 2) * 1024],
                    t["r_b"][:], -1.0), inc="dve")
        p.wait("sync", "dve")
        p.run("sync", lambda e, qk=qk, oc=oc: e.dma_start(
            out=contrib[oc * 128:(oc + 1) * 128, :], in_=qk), inc="st", n=16)
        dcons[oc] = p.cnt["st"]


def emit_qk_apply(p, t, din, contrib):
    psq_a, psq_b, psk_a, psk_b = t["ps_aq"], t["ps_bq"], t["ps_ak"], t["ps_bk"]
    p.wait("tensor", "dve")
    for ps, row in ((psq_a, 0), (psq_b, 1), (psk_a, 2), (psk_b, 3)):
        for th in range(2):
            p.run("tensor", lambda e, ps=ps, row=row, th=th: e.matmul(
                ps[:, th * 512:th * 512 + 512], t["ones1_f"][:],
                t["ab4"][0:1, row * 1024 + th * 512:row * 1024 + th * 512 + 512],
                start=True, stop=True), inc=("pe" if th == 1 else None))
    done = {}
    for ch in range(64):
        isq = ch < 32
        psa = psq_a if isq else psk_a
        psb = psq_b if isq else psk_b
        wln = t["qlnw"] if isq else t["klnw"]
        bln = t["qlnb"] if isq else t["klnb"]
        col = ch % 32
        buf = t["qk_db"][:, (ch % 2) * 1024:(ch % 2) * 1024 + 1024]
        if ch >= 2:
            p.wait("sync", "st", done[ch - 2])
        p.run("sync", lambda e, buf=buf, ch=ch: e.dma_start(
            out=buf, in_=contrib[ch * 128:(ch + 1) * 128, :]), inc="ld", n=16)
        p.wait("vector", "ld")
        p.wait("vector", "pe")
        p.run("vector", lambda e, buf=buf, psa=psa, wln=wln, col=col:
              e.scalar_tensor_tensor(t["t1_f"][:], buf, wln[:, col:col + 1], psa[:],
                                     ALU.mult, ALU.mult))
        p.run("vector", lambda e, psb=psb, wln=wln, col=col:
              e.scalar_tensor_tensor(t["t2_f"][:], psb[:], wln[:, col:col + 1],
                                     t["t1_f"][:], ALU.mult, ALU.add))
        if isq:
            p.run("vector", lambda e, buf=buf, bln=bln, col=col: e.tensor_scalar(
                buf, t["t2_f"][:], bln[:, col:col + 1], RS128, ALU.add, ALU.mult),
                inc="dve")
        else:
            p.run("vector", lambda e, buf=buf, bln=bln, col=col: e.tensor_scalar(
                buf, t["t2_f"][:], bln[:, col:col + 1], None, ALU.add), inc="dve")
        p.wait("sync", "dve")
        p.run("sync", lambda e, buf=buf, ch=ch: e.dma_start(
            out=contrib[ch * 128:(ch + 1) * 128, :], in_=buf), inc="st", n=16)
        done[ch] = p.cnt["st"]


def emit_allgather(p, t, in_ap, out_ap):
    p.join()
    p.run("gpsimd", lambda e: e.collective_compute(
        "AllGather", ALU.bypass,
        replica_groups=[[0, 1, 2, 3], [4, 5, 6, 7]],
        ins=[in_ap.opt()], outs=[out_ap.opt()]), inc="cc", n=1)
    for e in ENGS:
        p.wait(e, "cc")


def emit_flash(p, t, qkv_all, attn_contrib):
    p.wait("vector", "pool")
    p.wait("tensor", "pool")
    p.run("sync", lambda e: e.dma_start(out=t["slopes1"][:], in_=t["din_slopes"]),
          inc="ld", n=16)
    p.wait("tensor", "ld")
    p.run("tensor", lambda e: e.matmul(t["ps_zb"][:, 0:8], t["ones1_f"][:],
                                       t["slopes1"][0:1, :], start=True, stop=True),
          inc="pe")
    p.wait("scalar", "pe")
    p.run("scalar", lambda e: e.copy(t["slope_b"][:], t["ps_zb"][:, 0:8]),
          inc="act")
    kcons = {}
    for h in range(8):
        if h >= 1:
            p.wait("sync", "pe", kcons[h - 1])

        def body(e, j, h=h):
            hg = 8 * j + h
            for rr in range(4):
                e.dma_start(out=t["k_sb"][:, rr * 1024:(rr + 1) * 1024],
                            in_=qkv_all[rr * 12288 + 4096 + hg * 128:
                                        rr * 12288 + 4096 + hg * 128 + 128, :]
                            ).then_inc(p.sem["ld"], 16)
                e.dma_start(out=t["v_stage"][:, rr * 1024:(rr + 1) * 1024],
                            in_=qkv_all[rr * 12288 + 8192 + hg * 128:
                                        rr * 12288 + 8192 + hg * 128 + 128, :]
                            ).then_inc(p.sem["ld"], 16)
            for qc in range(8):
                e.dma_start(out=t["q_sb"][:, qc * 512:(qc + 1) * 512],
                            in_=qkv_all[(qc // 2) * 12288 + hg * 128:
                                        (qc // 2) * 12288 + hg * 128 + 128,
                                        (qc % 2) * 512:(qc % 2) * 512 + 512]
                            ).then_inc(p.sem["ld"], 16)
        p.run_switch("sync", [lambda e, j=j, b=body: b(e, j) for j in range(4)],
                     incs=[("ld", 16 * 16)])
        p.wait("tensor", "ld")
        p.wait("tensor", "dve")
        for tau in range(32):
            p.run("tensor", lambda e, tau=tau: e.transpose(
                t["ps_trv"][:], t["v_stage"][:, tau * 128:(tau + 1) * 128],
                t["ident_bf"][:]), inc="pe")
            p.wait("vector", "pe")
            p.run("vector", lambda e, tau=tau: e.tensor_copy(
                t["v_row"][:, tau * 128:(tau + 1) * 128], t["ps_trv"][:]), inc="dve")
            p.wait("tensor", "dve")
        if h == 0 and "fk" in t:
            p.wait("sync", "dve")
            p.run("sync", lambda e: e.dma_start(out=t["fk"], in_=t["k_sb"][:]), inc="st", n=16)
            p.run("sync", lambda e: e.dma_start(out=t["fq"], in_=t["q_sb"][:]), inc="st", n=16)
            p.run("sync", lambda e: e.dma_start(out=t["fvr"], in_=t["v_row"][:]), inc="st", n=16)
            p.wait("sync", "st")
        p.wait("vector", "act")
        p.run("vector", lambda e, h=h: e.tensor_scalar(
            t["bv_all"][:], t["iota_bv"][:], t["slope_b"][:, h:h + 1],
            -CSHIFT, ALU.mult, ALU.add))
        p.run("vector", lambda e, h=h: e.tensor_scalar(
            t["fvec"][:], t["iota_f"][:], t["slope_b"][0:1, h:h + 1], -1.0,
            ALU.mult, ALU.mult), inc="dve")
        p.wait("scalar", "dve")
        p.wait("tensor", "dve")
        for qc in range(8):
            nkt = 4 * qc + 4
            s_consumed = {}
            for kt in range(nkt):
                par = kt % 2
                if kt >= 2:
                    sem_, val_ = s_consumed[kt - 2]
                    p.wait("tensor", sem_, val_)
                p.run("tensor", lambda e, qc=qc, kt=kt, par=par: e.matmul(
                    t["ps_S"][:, par * 512:par * 512 + 512],
                    t["k_sb"][:, kt * 128:(kt + 1) * 128],
                    t["q_sb"][:, qc * 512:(qc + 1) * 512], start=True, stop=False))
                p.run("tensor", lambda e, par=par: e.matmul(
                    t["ps_S"][:, par * 512:par * 512 + 512],
                    t["ones1_bf"][:], t["fvec"][:], start=False, stop=True),
                    inc="pe")
                scnt = p.cnt["pe"]
                m = kt - 4 * qc + 32
                if m >= 32:
                    p.wait("vector", "pe", scnt)
                    p.wait("vector", "act")
                    p.run("vector", lambda e, par=par, m=m: e.tensor_add(
                        t["arg_db"][:, par * 512:par * 512 + 512],
                        t["ps_S"][:, par * 512:par * 512 + 512],
                        t["m01"][:, (m - 32) * 512:(m - 31) * 512]), inc="dve")
                    s_consumed[kt] = ("dve", p.cnt["dve"])
                    p.wait("scalar", "dve", p.cnt["dve"])
                    p.run("scalar", lambda e, par=par, m=m: e.activation(
                        t["pt_db"][:, par * 512:par * 512 + 512],
                        t["arg_db"][:, par * 512:par * 512 + 512], AF.Exp,
                        bias=t["bv_all"][:, m:m + 1]), inc="act")
                else:
                    p.wait("scalar", "pe", scnt)
                    p.run("scalar", lambda e, par=par, m=m: e.activation(
                        t["pt_db"][:, par * 512:par * 512 + 512],
                        t["ps_S"][:, par * 512:par * 512 + 512], AF.Exp,
                        bias=t["bv_all"][:, m:m + 1]), inc="act")
                    s_consumed[kt] = ("act", p.cnt["act"])
                if h == 0 and qc == 0 and kt == 0 and "fpt" in t:
                    p.wait("sync", "act")
                    p.run("sync", lambda e, par=par: e.dma_start(
                        out=t["fpt"], in_=t["pt_db"][:, par * 512:par * 512 + 512]),
                        inc="st", n=16)
                    p.run("sync", lambda e: e.dma_start(out=t["fbv"], in_=t["bv_all"][:]),
                        inc="st", n=16)
                    p.wait("sync", "st")
                p.wait("tensor", "act")
                p.run("tensor", lambda e, kt=kt, par=par, first=(kt == 0),
                      last=(kt == nkt - 1): e.matmul(
                    t["ps_o"][:], t["v_row"][:, kt * 128:(kt + 1) * 128],
                    t["pt_db"][:, par * 512:par * 512 + 512],
                    start=first, stop=last))
                p.run("tensor", lambda e, kt=kt, par=par, first=(kt == 0),
                      last=(kt == nkt - 1): e.matmul(
                    t["ps_z"][:], t["ones_bf"][:],
                    t["pt_db"][:, par * 512:par * 512 + 512],
                    start=first, stop=last), inc="pe")
            p.wait("vector", "pe")
            p.wait("vector", "st")
            p.run("vector", lambda e: e.tensor_copy(t["z_s"][:], t["ps_z"][0:1, :]))
            p.run("vector", lambda e: e.reciprocal(t["rz_fl"][:], t["z_s"][:]),
                  inc="dve")
            p.wait("tensor", "dve")
            p.run("tensor", lambda e: e.matmul(
                t["ps_zb"][:, 0:512], t["ones1_f"][:], t["rz_fl"][:],
                start=True, stop=True), inc="pe")
            p.wait("vector", "pe")
            p.run("vector", lambda e: e.tensor_copy(t["o_f"][:], t["ps_o"][:]))
            p.run("vector", lambda e: e.tensor_mul(t["att_bf"][:], t["o_f"][:],
                                                   t["ps_zb"][:, 0:512]), inc="dve")
            p.wait("sync", "dve")
            p.run("sync", lambda e, h=h, qc=qc: e.dma_start(
                out=attn_contrib[h * 128:(h + 1) * 128,
                                 qc * 512:(qc + 1) * 512],
                in_=t["att_bf"][:]), inc="st", n=16)
            p.wait("tensor", "dve")
        kcons[h] = p.cnt["pe"]


def emit_l1_out(p, t, din, xs_dram, attn_all, x1_dram, out_dram):
    for k in range(NC32):
        p.run("sync", lambda e, k=k: e.dma_start(
            out=t["res_a"][:, k * 1024:(k + 1) * 1024],
            in_=xs_dram[k * 128:(k + 1) * 128, :]), inc="ld", n=16)

    def body_att(e, j):
        for k in range(NC32):
            e.dma_start(out=t["res_b"][:, k * 1024:(k + 1) * 1024],
                        in_=attn_all[k * 128:(k + 1) * 128,
                                     j * 1024:(j + 1) * 1024]
                        ).then_inc(p.sem["ld"], 16)
    p.run_switch("sync", [lambda e, j=j, b=body_att: b(e, j) for j in range(4)],
                 incs=[("ld", 16 * NC32)])
    p.run("gpsimd", lambda e: e.dma_start(out=t["ocs_sb"][:], in_=din["ocs"]),
          inc="wld", n=16)
    p.run("sync", lambda e: e.dma_start(out=t["l1bias"][:], in_=din["l1bias_c"]),
          inc="ld", n=16)
    psA, psB, psos = t["ps_A"], t["ps_B"], t["ps_os"]
    wcons = {}
    xcons = {}
    for oc in range(NC32):
        bufA = t["w_db"][:, (oc % 2) * 4096:(oc % 2) * 4096 + 4096]
        bufB = t["w_db2"][:, (oc % 2) * 4096:(oc % 2) * 4096 + 4096]
        if oc >= 2:
            p.wait("gpsimd", "pe", wcons[oc - 2])
        p.run("gpsimd", lambda e, bufA=bufA, oc=oc: e.dma_start(
            out=bufA, in_=din["wout_r"][oc]), inc="wld", n=16)
        p.run("gpsimd", lambda e, bufB=bufB, oc=oc: e.dma_start(
            out=bufB, in_=din["signw_r"][oc]), inc="wld", n=16)
        p.wait("scalar", "wld")
        p.run("scalar", lambda e, bufB=bufB: e.activation(bufB, bufB, AF.Sign),
              inc="act")
        xb = t["xc_f"][:, (oc % 2) * 1024:(oc % 2) * 1024 + 1024]
        if oc >= 2:
            p.wait("sync", "dve", xcons[oc - 2])
        p.run("sync", lambda e, xb=xb, oc=oc: e.dma_start(
            out=xb, in_=x1_dram[oc * 128:(oc + 1) * 128, :]), inc="ld", n=16)
        p.wait("tensor", "wld")
        p.wait("tensor", "ld")
        p.wait("tensor", "act")
        p.wait("tensor", "dve")
        for th in range(2):
            for k in range(NC32):
                p.run("tensor", lambda e, bufA=bufA, k=k, th=th: e.matmul(
                    psA[:, th * 512:th * 512 + 512], bufA[:, k * 128:(k + 1) * 128],
                    t["res_b"][:, k * 1024 + th * 512:k * 1024 + th * 512 + 512],
                    start=(k == 0), stop=(k == NC32 - 1)))
            for k in range(NC32):
                p.run("tensor", lambda e, bufB=bufB, k=k, th=th: e.matmul(
                    psB[:, th * 512:th * 512 + 512], bufB[:, k * 128:(k + 1) * 128],
                    t["res_a"][:, k * 1024 + th * 512:k * 1024 + th * 512 + 512],
                    start=(k == 0), stop=(k == NC32 - 1)))
        for th in range(2):
            p.run("tensor", lambda e, oc=oc, th=th: e.matmul(
                psos[:, th * 512:th * 512 + 512],
                t["ocs_sb"][:, oc * 128:(oc + 1) * 128],
                t["routT"][:, th * 512:th * 512 + 512], start=True, stop=True),
                inc="pe")
        wcons[oc] = p.cnt["pe"]
        p.wait("vector", "pe")
        p.wait("vector", "ld")
        p.wait("vector", "st")
        p.run("vector", lambda e: e.tensor_copy(t["os_f"][:], psos[:]))
        p.run("vector", lambda e: e.tensor_mul(t["t1_f"][:], t["os_f"][:], psB[:]),
              inc="dve")
        p.wait("scalar", "dve")
        p.run("scalar", lambda e, oc=oc: e.activation(
            t["g_sb"][:], t["t1_f"][:], AF.Silu, bias=t["l1bias"][:, oc:oc + 1]),
            inc="act")
        p.wait("vector", "act")
        p.run("vector", lambda e: e.tensor_mul(t["t2_f"][:], t["g_sb"][:], psA[:]))
        p.run("vector", lambda e, xb=xb: e.tensor_add(t["t2_f"][:], t["t2_f"][:], xb),
              inc="dve")
        xcons[oc] = p.cnt["dve"]
        p.wait("sync", "dve")
        p.run("sync", lambda e, oc=oc: e.dma_start(
            out=out_dram[oc * 128:(oc + 1) * 128, :], in_=t["t2_f"][:]),
            inc="st", n=16)



def build_program():
    nc = bass.Bass("TRN2", target_bir_lowering=False)
    din = {}

    def ext(name, shape):
        din[name] = nc.dram_tensor(name, shape, F32, kind="ExternalInput").ap()

    ext("x0T", [D, T])
    ext("wq_r", [128, NC32 * 4096])
    ext("bq", [128, 32])
    ext("bk", [128, 1])
    ext("bv", [128, 1])
    ext("wk_r", [128, NC32 * 128])
    ext("wv_r", [128, NC32 * 128])
    l0w = nc.dram_tensor("l0w_r", [NC32, 128, NC32 * 128], F32, kind="ExternalInput").ap()
    din["l0w_r"] = [l0w[i] for i in range(NC32)]
    ext("l0bias_c", [128, 32])
    ext("aT_r", [128, NC32 * 256])
    ext("bfT_r", [128, NC32 * 256])
    ext("coeffs_rep", [128, 2])
    ext("scales_rep", [128, 2])
    wqkv = nc.dram_tensor("wqkv_r", [96, 128, NC32 * 128], F32, kind="ExternalInput").ap()
    din["wqkv_r"] = [wqkv[i] for i in range(96)]
    ext("gate_r", [128, NC32 * 8])
    ext("ics", [8, 4096])
    ext("ocs", [8, 4096])
    ext("qlnw_c", [128, 32])
    ext("qlnb_c", [128, 32])
    ext("klnw_c", [128, 32])
    ext("klnb_c", [128, 32])
    wout = nc.dram_tensor("wout_r", [NC32, 128, NC32 * 128], F32, kind="ExternalInput").ap()
    din["wout_r"] = [wout[i] for i in range(NC32)]
    signw = nc.dram_tensor("signw_r", [NC32, 128, NC32 * 128], F32, kind="ExternalInput").ap()
    din["signw_r"] = [signw[i] for i in range(NC32)]
    ext("l1bias_c", [128, 32])
    ext("slopes", [1, 8])

    x1T = nc.dram_tensor("x1T", [D, T], F32, kind="Internal").ap()
    xsT_d = nc.dram_tensor("xsT_d", [D, T], BF16, kind="Internal").ap()
    qkv_contrib = nc.dram_tensor("qkv_contrib", [3 * D, T], BF16, kind="Internal").ap()
    qkv_all = nc.dram_tensor("qkv_all", [4 * 3 * D, T], BF16, kind="Internal").ap()
    attn_contrib = nc.dram_tensor("attn_contrib", [T, 4096], BF16, kind="Internal").ap()
    attn_all = nc.dram_tensor("attn_all", [D, 4096], BF16, kind="Internal").ap()
    out_xT = nc.dram_tensor("out_xT", [D, T], F32, kind="ExternalOutput").ap()
    dbg = {}
    for name in DEBUG_DUMPS:
        shp = {"xn0": [128, NC32 * 1024], "qsT": [128, 1024], "kT": [128, 1024],
               "vT": [128, 1024], "oT": [128, 1024], "wqsum": [128, NC32 * 128],
               "xa": [128, 2048], "gall": [D, T], "oball": [D, T],
               "sel": [128, 32 * 128], "oTd": [128, 1024], "uall": [D, T],
               "xn1": [128, NC32 * 1024], "rout": [8, 1024],
               "xsd": [D, T], "qkvc": [3 * D, T], "attc": [T, 4096],
               "fk": [128, 4096], "fq": [128, 4096], "fvr": [128, 4096],
               "fpt": [128, 512], "fbv": [128, 36]}[name]
        dbg[name] = nc.dram_tensor("dbg_" + name, shp,
                                   BF16 if name in ("xn0", "oT", "xa", "wqsum", "sel", "oTd", "xn1",
                                                    "rout", "xsd", "qkvc", "attc", "fk", "fq", "fvr", "fpt") else F32,
                                   kind="ExternalOutput").ap()

    p = Prog(nc)
    with ExitStack() as s:
        for sem in ("ld", "wld", "pe", "dve", "act", "pool", "st", "cc"):
            p.new_sem(s, sem)
        sb = lambda n, shape, dt: s.enter_context(nc.sbuf_tensor(n, shape, dt)).ap()
        t = {}
        t["ones_bf"] = sb("ones_bf", [128, 1], BF16)
        t["ones1_f"] = sb("ones1_f", [1, 128], F32)
        t["ident_bf"] = sb("ident_bf", [128, 128], BF16)
        t["iota_vec"] = sb("iota_vec", [128, 1], F32)
        t["cval"] = sb("cval", [128, 8], F32)
        t["routT"] = sb("routT", [8, 1024], BF16)
        t["qlnw"] = sb("qlnw", [128, 32], F32)
        t["qlnb"] = sb("qlnb", [128, 32], F32)
        t["klnw"] = sb("klnw", [128, 32], F32)
        t["klnb"] = sb("klnb", [128, 32], F32)
        t["l1bias"] = sb("l1bias", [128, 32], F32)
        t["ones8_f"] = sb("ones8_f", [8, 1], F32)
        t["ones1_bf"] = sb("ones1_bf", [1, 128], BF16)
        t["iota_f"] = sb("iota_f", [1, 512], F32)
        t["res_a"] = sb("res_a", [128, NC32 * 1024], BF16)
        t["sq_bf"] = sb("sq_bf", [128, 2 * 1024], BF16)
        t["xc_f"] = sb("xc_f", [128, 2 * 1024], F32)
        t["t1_f"] = sb("t1_f", [128, 1024], F32)
        t["t2_f"] = sb("t2_f", [128, 1024], F32)
        t["g_sb"] = sb("g_sb", [128, 1024], F32)
        t["w_db"] = sb("w_db", [128, 2 * 4096], BF16)
        t["l0bias"] = sb("l0bias", [128, 32], F32)

        p.init_j("sync", mybir.EngineType.SP)
        emit_consts(p, t)

        with ExitStack() as sln:
            pss = lambda n, shape: sln.enter_context(nc.psum_tensor(n, shape, F32)).ap()
            t["ps_st"] = pss("ps_st0", [1, 2048])
            t["ps_a"] = pss("ps_a0", [128, 1024])
            t["ps_b"] = pss("ps_b0", [128, 1024])
            sbl = lambda n, shape, dt: sln.enter_context(nc.sbuf_tensor(n, shape, dt)).ap()
            t["xb_bf"] = sbl("xb_bf0", [128, 2048], BF16)
            t["r_mu"] = sbl("r_mu0", [1, 1024], F32)
            t["r_var"] = sbl("r_var0", [1, 1024], F32)
            t["r_a"] = sbl("r_a0", [1, 1024], F32)
            t["r_b"] = sbl("r_b0", [1, 1024], F32)
            if "ln" in PHASES:
                emit_ln(p, t, din["x0T"], t["res_a"])
            p.join()
        if "xn0" in dbg:
            p.run("sync", lambda e: e.dma_start(out=dbg["xn0"], in_=t["res_a"][:]),
                  inc="st", n=16)

        with ExitStack() as sl0:
            sb0 = lambda n, shape, dt: sl0.enter_context(nc.sbuf_tensor(n, shape, dt)).ap()
            ps0 = lambda n, shape: sl0.enter_context(nc.psum_tensor(n, shape, F32)).ap()
            t["sel"] = sb0("sel", [128, 32 * 128], BF16)
            t["ident_f"] = sb0("ident_f", [128, 128], F32)
            emit_l0_consts(p, t)
            t["qs_row"] = sb0("qs_row", [128, 1024], F32)
            t["k_row"] = sb0("k_row", [128, 1024], F32)
            t["v_row"] = sb0("v_row", [128, 1024], F32)
            t["oT_bf"] = sb0("oT_bf", [128, 1024], BF16)
            t["ps_tr"] = ps0("ps_tr0", [128, 128])
            with ExitStack() as s1:
                sb1 = lambda n, shape, dt: s1.enter_context(nc.sbuf_tensor(n, shape, dt)).ap()
                ps1 = lambda n, shape: s1.enter_context(nc.psum_tensor(n, shape, F32)).ap()
                t["wqsum"] = sb1("wqsum", [128, NC32 * 128], BF16)
                t["wq_red"] = sb1("wq_red", [128, 128], F32)
                t["wk_sb"] = sb1("wk_sb", [128, NC32 * 128], BF16)
                t["wv_sb"] = sb1("wv_sb", [128, NC32 * 128], BF16)
                t["bq32"] = sb1("bq32", [128, 32], F32)
                t["bkv"] = sb1("bkv", [128, 3], F32)
                t["qsT"] = sb1("qsT", [128, 1024], F32)
                t["kT"] = sb1("kT", [128, 1024], F32)
                t["vT"] = sb1("vT", [128, 1024], F32)
                t["ps_q"] = ps1("ps_q", [128, 1024])
                t["ps_k"] = ps1("ps_k", [128, 1024])
                t["ps_v"] = ps1("ps_v", [128, 1024])
                if "attnprep" in PHASES:
                    emit_l0_attnprep(p, t, din)
                if "wqsum" in dbg:
                    p.wait("sync", "dve")
                    p.run("sync", lambda e: e.dma_start(
                        out=dbg["wqsum"], in_=t["wqsum"][:]), inc="st", n=16)
                for name in ("qsT", "kT", "vT"):
                    if name in dbg:
                        p.wait("sync", "dve")
                        p.run("sync", lambda e, name=name: e.dma_start(
                            out=dbg[name], in_=t[name][:]), inc="st", n=16)
                p.join()
            with ExitStack() as s2:
                sb2 = lambda n, shape, dt: s2.enter_context(nc.sbuf_tensor(n, shape, dt)).ap()
                t["exp_db"] = sb2("exp_db", [128, 2 * 128], F32)
                t["dump_f"] = sb2("dump_f", [128, 128], F32)
                t["zrow"] = sb2("zrow", [128, 128], F32)
                t["numrow"] = sb2("numrow", [128, 128], F32)
                t["rz"] = sb2("rz", [128, 128], F32)
                t["orow"] = sb2("orow", [128, 128], F32)
                if "softmax" in PHASES:
                    emit_l0_softmax(p, t)
                p.join()
            if "oT" in dbg:
                p.run("sync", lambda e: e.dma_start(out=dbg["oT"], in_=t["oT_bf"][:]),
                      inc="st", n=16)
            with ExitStack() as s3:
                sb3 = lambda n, shape, dt: s3.enter_context(nc.sbuf_tensor(n, shape, dt)).ap()
                ps3 = lambda n, shape: s3.enter_context(nc.psum_tensor(n, shape, F32)).ap()
                t["aT_sb"] = sb3("aT_sb", [128, NC32 * 256], BF16)
                t["xaT_bf"] = sb3("xaT_bf", [128, 2048], BF16)
                t["crep"] = sb3("crep", [128, 2], F32)
                t["srep"] = sb3("srep", [128, 2], F32)
                t["wer"] = sb3("wer", [128, 2], F32)
                with ExitStack() as s3a:
                    t["ps_xa"] = s3a.enter_context(nc.psum_tensor("ps_xa", [128, 2048], F32)).ap()
                    if "xa" in PHASES:
                        emit_l0_xa(p, t, din)
                    if "xa" in dbg:
                        p.wait("sync", "dve")
                        p.run("sync", lambda e: e.dma_start(
                            out=dbg["xa"], in_=t["xaT_bf"][:]), inc="st", n=16)
                    p.join()
                t["bfT_sb"] = sb3("bfT_sb", [128, NC32 * 256], BF16)
                t["ps_g"] = ps3("ps_g", [128, 1024])
                t["ps_ob"] = ps3("ps_ob", [128, 1024])
                if "sel" in dbg:
                    p.wait("sync", "pool")
                    p.run("sync", lambda e: e.dma_start(out=dbg["sel"], in_=t["sel"][:]),
                          inc="st", n=16)
                if "oTd" in dbg:
                    p.run("sync", lambda e: e.dma_start(out=dbg["oTd"], in_=t["oT_bf"][:]),
                          inc="st", n=16)
                if "gall" in dbg:
                    t["gall"] = dbg["gall"]
                    t["oball"] = dbg["oball"]
                if "uall" in dbg:
                    t["uall"] = dbg["uall"]
                if "ffn" in PHASES:
                    emit_l0_ffn_update(p, t, din, din["x0T"], x1T)
                p.join()

        p.run("sync", lambda e: e.dma_start(out=t["qlnw"][:], in_=din["qlnw_c"]), inc="ld", n=16)
        p.run("sync", lambda e: e.dma_start(out=t["qlnb"][:], in_=din["qlnb_c"]), inc="ld", n=16)
        p.run("sync", lambda e: e.dma_start(out=t["klnw"][:], in_=din["klnw_c"]), inc="ld", n=16)
        p.run("sync", lambda e: e.dma_start(out=t["klnb"][:], in_=din["klnb_c"]), inc="ld", n=16)
        p.run("sync", lambda e: e.dma_start(out=t["l1bias"][:], in_=din["l1bias_c"]), inc="ld", n=16)
        with ExitStack() as sln1:
            sbl1 = lambda n, shape, dt: sln1.enter_context(nc.sbuf_tensor(n, shape, dt)).ap()
            t["xb_bf"] = sbl1("xb_bf1", [128, 2048], BF16)
            t["r_mu"] = sbl1("r_mu1", [1, 1024], F32)
            t["r_var"] = sbl1("r_var1", [1, 1024], F32)
            t["r_a"] = sbl1("r_a1", [1, 1024], F32)
            t["r_b"] = sbl1("r_b1", [1, 1024], F32)
            t["ab4"] = sbl1("ab4", [1, 4096], F32)
            t["qk_db"] = sbl1("qk_db", [128, 2 * 1024], BF16)
            with ExitStack() as s4:
                pss1 = lambda n, shape: s4.enter_context(nc.psum_tensor(n, shape, F32)).ap()
                t["ps_st"] = pss1("ps_st1", [1, 2048])
                t["ps_a"] = pss1("ps_a1", [128, 1024])
                t["ps_b"] = pss1("ps_b1", [128, 1024])
                emit_ln(p, t, x1T, t["res_a"])
                p.join()
            if "xn1" in dbg:
                p.run("sync", lambda e: e.dma_start(out=dbg["xn1"], in_=t["res_a"][:]),
                      inc="st", n=16)
            with ExitStack() as s5:
                sb5 = lambda n, shape, dt: s5.enter_context(nc.sbuf_tensor(n, shape, dt)).ap()
                ps5 = lambda n, shape: s5.enter_context(nc.psum_tensor(n, shape, F32)).ap()
                t["gate_sb"] = sb5("gate_sb", [128, NC32 * 8], BF16)
                t["ics_sb"] = sb5("ics_sb", [8, 4096], BF16)
                t["expr8"] = sb5("expr8", [8, 1024], F32)
                t["rz8"] = sb5("rz8", [1, 1024], F32)
                t["xs_bf"] = sb5("xs_bf", [128, 1024], BF16)
                t["ps_r8"] = ps5("ps_r8", [8, 1024])
                t["ps_z8"] = ps5("ps_z8", [1, 1024])
                t["ps_zb8"] = ps5("ps_zb8", [8, 1024])
                t["ps_is"] = ps5("ps_is", [128, 1024])
                emit_router_xs(p, t, din, xsT_d)
                p.join()
                if "rout" in dbg:
                    p.run("sync", lambda e: e.dma_start(out=dbg["rout"], in_=t["routT"][:]),
                          inc="st", n=16)
                if "xsd" in dbg:
                    p.run("sync", lambda e: e.dma_start(out=dbg["xsd"], in_=xsT_d),
                          inc="st", n=16)
            with ExitStack() as s6:
                ps6 = lambda n, shape: s6.enter_context(nc.psum_tensor(n, shape, F32)).ap()
                t["ps_qk"] = ps6("ps_qk", [128, 1024])
                t["ps_st4"] = ps6("ps_st4", [1, 2048])
                emit_l1_qkv(p, t, din, qkv_contrib)
                p.join()
            with ExitStack() as s7:
                ps7 = lambda n, shape: s7.enter_context(nc.psum_tensor(n, shape, F32)).ap()
                t["ps_aq"] = ps7("ps_aq", [128, 1024])
                t["ps_bq"] = ps7("ps_bq", [128, 1024])
                t["ps_ak"] = ps7("ps_ak", [128, 1024])
                t["ps_bk"] = ps7("ps_bk", [128, 1024])
                emit_qk_apply(p, t, din, qkv_contrib)
                p.join()
                if "qkvc" in dbg:
                    p.run("sync", lambda e: e.dma_start(out=dbg["qkvc"], in_=qkv_contrib),
                          inc="st", n=16)
                    p.wait("sync", "st")
        emit_allgather(p, t, qkv_contrib, qkv_all)
        with ExitStack() as s9:
            sb9 = lambda n, shape, dt: s9.enter_context(nc.sbuf_tensor(n, shape, dt)).ap()
            ps9 = lambda n, shape, dt=F32: s9.enter_context(nc.psum_tensor(n, shape, dt)).ap()
            t["m01"] = sb9("m01", [128, 4 * 512], F32)
            t["iota_bv"] = sb9("iota_bv", [128, 36], F32)
            emit_m01(p, t)
            t["k_sb"] = sb9("k_sb", [128, 4096], BF16)
            t["v_stage"] = sb9("v_stage", [128, 4096], BF16)
            t["v_row"] = sb9("v_row_fl", [128, 4096], BF16)
            t["q_sb"] = sb9("q_sb", [128, 4096], BF16)
            t["pt_db"] = sb9("pt_db", [128, 1024], BF16)
            t["arg_db"] = sb9("arg_db", [128, 1024], F32)
            t["fvec"] = sb9("fvec", [1, 512], BF16)
            t["bv_all"] = sb9("bv_all", [128, 36], F32)
            t["slope_b"] = sb9("slope_b", [128, 8], F32)
            t["slopes1"] = sb9("slopes1", [1, 8], F32)
            t["z_s"] = sb9("z_s", [1, 512], F32)
            t["rz_fl"] = sb9("rz_fl", [1, 512], F32)
            t["o_f"] = sb9("o_f", [128, 512], F32)
            t["att_bf"] = sb9("att_bf", [128, 512], BF16)
            t["ps_S"] = ps9("ps_S", [128, 1024])
            t["ps_o"] = ps9("ps_o", [128, 512])
            t["ps_z"] = ps9("ps_z", [1, 512])
            t["ps_zb"] = ps9("ps_zb", [128, 512])
            t["ps_trv"] = ps9("ps_trv", [128, 128], BF16)
            t["din_slopes"] = din["slopes"]
            for nm in ("fk", "fq", "fvr", "fpt", "fbv"):
                if nm in dbg:
                    t[nm] = dbg[nm]
            emit_flash(p, t, qkv_all, attn_contrib)
            p.join()
            if "attc" in dbg:
                p.run("sync", lambda e: e.dma_start(out=dbg["attc"], in_=attn_contrib),
                      inc="st", n=16)
                p.wait("sync", "st")
        emit_allgather(p, t, attn_contrib, attn_all)
        with ExitStack() as s10:
            sb10 = lambda n, shape, dt: s10.enter_context(nc.sbuf_tensor(n, shape, dt)).ap()
            ps10 = lambda n, shape: s10.enter_context(nc.psum_tensor(n, shape, F32)).ap()
            t["res_b"] = sb10("res_b", [128, NC32 * 1024], BF16)
            t["w_db2"] = sb10("w_db2", [128, 2 * 4096], BF16)
            t["ocs_sb"] = sb10("ocs_sb", [8, 4096], BF16)
            t["os_f"] = t["t2_f"]
            t["ps_A"] = ps10("ps_A", [128, 1024])
            t["ps_B"] = ps10("ps_B", [128, 1024])
            t["ps_os"] = ps10("ps_os", [128, 1024])
            emit_l1_out(p, t, din, xsT_d, attn_all, x1T, out_xT)
            p.join()
        p.wait("sync", "st")

        with nc.Block() as block:
            p.emit(block)
    return nc



def _prep_weights(inputs):
    g = {k: np.asarray(v, dtype=np.float32) for k, v in inputs.items()}
    out = {}
    out["wq_r"] = np.ascontiguousarray(
        g["L0_attn_Wq"].reshape(32, 128, 4096).transpose(1, 0, 2)).reshape(128, -1)
    out["bq"] = np.ascontiguousarray(g["L0_attn_bq"])
    out["bk"] = np.ascontiguousarray(g["L0_attn_bk"])
    out["bv"] = np.ascontiguousarray(g["L0_attn_bv"])
    out["wk_r"] = np.ascontiguousarray(
        g["L0_attn_Wk"][..., 0].reshape(32, 128, 128).transpose(1, 0, 2)).reshape(128, -1)
    out["wv_r"] = np.ascontiguousarray(
        g["L0_attn_Wv"][..., 0].reshape(32, 128, 128).transpose(1, 0, 2)).reshape(128, -1)
    WT = np.ascontiguousarray(g["L0_ffn_W"].T)
    out["l0w_r"] = np.ascontiguousarray(
        WT.reshape(32, 128, 32, 128).transpose(2, 1, 0, 3)).reshape(32, 128, -1)
    out["l0bias_c"] = np.ascontiguousarray(g["L0_ffn_bias"].reshape(32, 128).T)
    AT = np.ascontiguousarray(g["L0_ffn_A"].transpose(2, 0, 1).reshape(4096, 256))
    out["aT_r"] = np.ascontiguousarray(
        AT.reshape(32, 128, 256).transpose(1, 0, 2)).reshape(128, -1)
    BfT = np.ascontiguousarray(g["L0_ffn_Bf"].transpose(0, 2, 1).reshape(256, 4096))
    out["bfT_r"] = np.ascontiguousarray(
        BfT.reshape(2, 128, 32, 128).transpose(1, 2, 0, 3)).reshape(128, -1)
    out["coeffs_rep"] = np.ascontiguousarray(
        np.repeat(g["L0_ffn_coeffs"], 16).reshape(2, 128).T)
    out["scales_rep"] = np.ascontiguousarray(
        np.repeat(g["L0_ffn_scales"], 16).reshape(2, 128).T)
    QT = np.ascontiguousarray(g["L1_attn_w_qkv"].T)
    out["wqkv_r"] = np.ascontiguousarray(
        QT.reshape(32, 128, 96, 128).transpose(2, 1, 0, 3)).reshape(96, 128, -1)
    out["gate_r"] = np.ascontiguousarray(
        g["L1_ffn_gate_w"].T.reshape(32, 128, 8).transpose(1, 0, 2)).reshape(128, -1)
    out["ics"] = np.ascontiguousarray(g["L1_ffn_in_channel_scale"])
    out["ocs"] = np.ascontiguousarray(g["L1_ffn_out_channel_scale"])
    for nm, key in (("qlnw_c", "L1_attn_q_ln_w"), ("qlnb_c", "L1_attn_q_ln_b"),
                    ("klnw_c", "L1_attn_k_ln_w"), ("klnb_c", "L1_attn_k_ln_b")):
        out[nm] = np.ascontiguousarray(g[key].reshape(32, 128).T)
    WO = np.ascontiguousarray(g["L1_attn_w_out"].T)
    out["wout_r"] = np.ascontiguousarray(
        WO.reshape(32, 128, 32, 128).transpose(2, 1, 0, 3)).reshape(32, 128, -1)
    SW = np.ascontiguousarray(g["L1_ffn_weight"].T)
    out["signw_r"] = np.ascontiguousarray(
        SW.reshape(32, 128, 32, 128).transpose(2, 1, 0, 3)).reshape(32, 128, -1)
    out["l1bias_c"] = np.ascontiguousarray(g["L1_ffn_bias"].reshape(32, 128).T)
    return out


def _run(inputs):
    x = np.asarray(inputs["L0_attn_x"], dtype=np.float32)
    B, S, _ = x.shape
    w = _prep_weights(inputs)
    nc = build_program()
    in_maps = []
    for c in range(8):
        b, j = c // 4, c % 4
        xs = np.ascontiguousarray(x[b, j * 1024:(j + 1) * 1024, :].T)
        sl = np.asarray(ALIBI_SLOPES[8 * j:8 * j + 8], np.float32).reshape(1, 8)
        m = {"x0T": xs, "slopes": sl}
        m.update(w)
        in_maps.append(m)
    res = run_bass_kernel_spmd(nc, in_maps, core_ids=list(range(8)))
    out = np.empty((B, S, D), dtype=np.float32)
    for c in range(8):
        b, j = c // 4, c % 4
        out[b, j * 1024:(j + 1) * 1024, :] = res.results[c]["out_xT"].T
    return out, res


def kernel(**inputs):
    out, _ = _run(inputs)
    return out


# version 02 (final)
# speedup vs baseline: 1.1455x; 1.1455x over previous
import numpy as np
import ml_dtypes
import concourse.bass as bass
import concourse.mybir as mybir
from concourse.bass_utils import run_bass_kernel_spmd
from contextlib import ExitStack

F32 = mybir.dt.float32
BF16 = mybir.dt.bfloat16
AF = mybir.ActivationFunctionType
ALU = mybir.AluOpType
AX = mybir.AxisListType

D = 4096
T = 1024
NC32 = 32
EPS = 1e-5
RS128 = float(1.0 / np.sqrt(128.0))
CSHIFT = 10.0
ALIBI_SLOPES = [2.0 ** (-(i + 1) * 0.25) for i in range(32)]

ENGS = ("sync", "scalar", "vector", "tensor", "gpsimd")

DEBUG_DUMPS = []
PHASES = ("ln", "attnprep", "softmax", "xa", "ffn")
N_LAYERS = 2


class Prog:
    def __init__(self, nc):
        self.nc = nc
        self.q = {e: [] for e in ENGS}
        self.cnt = {}
        self.sem = {}
        self.waited = {e: {} for e in ENGS}

    def new_sem(self, stack, name):
        self.sem[name] = stack.enter_context(self.nc.semaphore(name))
        self.cnt[name] = 0

    def wait(self, eng, sem, val=None):
        if val is None:
            val = self.cnt[sem]
        if val <= 0 or self.waited[eng].get(sem, 0) >= val:
            return
        self.waited[eng][sem] = val
        h = self.sem[sem]
        self.q[eng].append(lambda e, h=h, v=val: e.wait_ge(h, v))

    def run(self, eng, fn, inc=None, n=1):
        if inc is None:
            self.q[eng].append(fn)
        else:
            self.cnt[inc] += n
            h = self.sem[inc]
            self.q[eng].append(lambda e, f=fn, h=h, n=n: f(e).then_inc(h, n))

    def join(self):
        snapshot = dict(self.cnt)
        for e in ENGS:
            for sname, v in snapshot.items():
                self.wait(e, sname, v)

    def emit(self, block):
        for e in ENGS:
            getattr(block, e)(lambda eng, q=self.q[e]: [f(eng) for f in q])

    def init_j(self, eng, etype):
        self._jbox = getattr(self, "_jbox", {})
        box = {}
        self._jbox[eng] = box
        nc = self.nc

        def go(e):
            pid = nc.partition_id(engines=[etype])
            box["j"] = e.snap(pid % 4)
        self.q[eng].append(go)

    def run_switch(self, eng, body_fns, incs):
        for s_, n_ in incs:
            self.cnt[s_] += n_
        box = self._jbox[eng]

        def go(e):
            for i in e.Switch(box["j"], 4):
                body_fns[i](e)
        self.q[eng].append(go)



def emit_consts(p, t):
    r = lambda fn: p.run("gpsimd", fn, inc="pool")
    r(lambda e: e.memset(t["ones_bf"][:], 1.0))
    r(lambda e: e.memset(t["ones1_f"][:], 1.0))
    r(lambda e: e.memset(t["ident_bf"][:], 0.0))
    r(lambda e: e.affine_select(t["ident_bf"][:], t["ident_bf"][:],
                                compare_op=ALU.not_equal, fill=1.0, base=0,
                                pattern=[[-1, 128]], channel_multiplier=1))
    r(lambda e: e.iota(t["iota_vec"][:], pattern=[[1, 1]], base=0,
                       channel_multiplier=1, allow_small_or_imprecise_dtypes=True))
    r(lambda e: e.memset(t["cval"][:, 0:1], EPS))
    r(lambda e: e.memset(t["cval"][:, 1:2], -0.5))
    r(lambda e: e.memset(t["ones8_f"][:], 1.0))
    r(lambda e: e.memset(t["ones1_bf"][:], 1.0))
    r(lambda e: e.iota(t["iota_f"][:], pattern=[[1, 512]], base=0,
                       channel_multiplier=0, allow_small_or_imprecise_dtypes=True))


def emit_l0_consts(p, t):
    r = lambda fn: p.run("gpsimd", fn, inc="pool")
    r(lambda e: e.memset(t["ident_f"][:], 0.0))
    r(lambda e: e.affine_select(t["ident_f"][:], t["ident_f"][:],
                                compare_op=ALU.not_equal, fill=1.0, base=0,
                                pattern=[[-1, 128]], channel_multiplier=1))
    for k in range(32):
        sl = t["sel"][:, k * 128:(k + 1) * 128]
        r(lambda e, sl=sl: e.memset(sl, 1.0))
        r(lambda e, sl=sl, k=k: e.affine_select(
            sl, sl, compare_op=ALU.is_equal, fill=0.0, base=-4 * k,
            pattern=[[-1, 4], [0, 32]], channel_multiplier=1))


def emit_m01(p, t):
    p.run("gpsimd", lambda e: e.iota(
        t["iota_bv"][:], pattern=[[128, 36]], base=-4096, channel_multiplier=1,
        allow_small_or_imprecise_dtypes=True), inc="pool")
    r = lambda fn: p.run("gpsimd", fn, inc="pool")
    for i in range(4):
        sl = t["m01"][:, i * 512:(i + 1) * 512]
        r(lambda e, sl=sl: e.memset(sl, 0.0))
        r(lambda e, sl=sl, i=i: e.affine_select(
            sl, sl, compare_op=ALU.is_ge, fill=-10000.0, base=-i * 128,
            pattern=[[1, 512]], channel_multiplier=-1))


def emit_ln(p, t, x_dram, out_res):
    pst, psa, psb = t["ps_st"], t["ps_a"], t["ps_b"]
    xbb, sqb, xc = t["xb_bf"], t["sq_bf"], t["xc_f"]
    mu, var, a_ln, b_ln = t["r_mu"], t["r_var"], t["r_a"], t["r_b"]
    p.wait("tensor", "pool")
    consumed = {}
    for k in range(NC32):
        buf = xbb[:, (k % 2) * 1024:(k % 2) * 1024 + 1024]
        if k >= 2:
            p.wait("gpsimd", "pe", consumed[k - 2])
        p.run("gpsimd", lambda e, buf=buf, k=k: e.dma_start(
            out=buf, in_=x_dram[k * 128:(k + 1) * 128, :]), inc="wld", n=16)
        p.wait("scalar", "wld")
        sq = sqb[:, (k % 2) * 1024:(k % 2) * 1024 + 1024]
        p.run("scalar", lambda e, buf=buf, sq=sq: e.activation(sq, buf, AF.Square),
              inc="act")
        p.wait("tensor", "act")
        for th in range(2):
            p.run("tensor", lambda e, buf=buf, th=th, k=k: e.matmul(
                pst[:, th * 512:th * 512 + 512], t["ones_bf"][:],
                buf[:, th * 512:th * 512 + 512],
                start=(k == 0), stop=(k == NC32 - 1)))
        for th in range(2):
            p.run("tensor", lambda e, sq=sq, th=th, k=k: e.matmul(
                pst[:, 1024 + th * 512:1024 + th * 512 + 512], t["ones_bf"][:],
                sq[:, th * 512:th * 512 + 512],
                start=(k == 0), stop=(k == NC32 - 1)),
                inc=("pe" if th == 1 else None))
        consumed[k] = p.cnt["pe"]
    p.wait("vector", "pe")
    p.run("vector", lambda e: e.tensor_scalar_mul(mu[:], pst[0:1, 0:1024], 1.0 / D))
    p.run("vector", lambda e: e.tensor_scalar_mul(var[:], pst[0:1, 1024:2048], 1.0 / D))
    p.run("vector", lambda e: e.tensor_mul(a_ln[:], mu[:], mu[:]))
    p.run("vector", lambda e: e.tensor_sub(var[:], var[:], a_ln[:]), inc="dve")
    p.wait("scalar", "dve")
    p.run("scalar", lambda e: e.activation(b_ln[:], var[:], AF.Ln,
                                           bias=t["cval"][0:1, 0:1]))
    p.run("scalar", lambda e: e.activation(a_ln[:], b_ln[:], AF.Exp,
                                           scale=t["cval"][0:1, 1:2]), inc="act")
    p.wait("vector", "act")
    p.run("vector", lambda e: e.tensor_mul(b_ln[:], mu[:], a_ln[:]))
    p.run("vector", lambda e: e.tensor_scalar_mul(b_ln[:], b_ln[:], -1.0), inc="dve")
    p.wait("tensor", "dve")
    for th in range(2):
        p.run("tensor", lambda e, th=th: e.matmul(
            psa[:, th * 512:th * 512 + 512], t["ones1_f"][:],
            a_ln[0:1, th * 512:th * 512 + 512], start=True, stop=True))
    for th in range(2):
        p.run("tensor", lambda e, th=th: e.matmul(
            psb[:, th * 512:th * 512 + 512], t["ones1_f"][:],
            b_ln[0:1, th * 512:th * 512 + 512], start=True, stop=True),
            inc=("pe" if th == 1 else None))
    done = {}
    for k in range(NC32):
        buf = xc[:, (k % 2) * 1024:(k % 2) * 1024 + 1024]
        if k >= 2:
            p.wait("sync", "dve", done[k - 2])
        p.run("sync", lambda e, buf=buf, k=k: e.dma_start(
            out=buf, in_=x_dram[k * 128:(k + 1) * 128, :]), inc="ld", n=16)
        p.wait("vector", "ld")
        p.wait("vector", "pe")
        dst = out_res[:, k * 1024:(k + 1) * 1024]
        p.run("vector", lambda e, buf=buf: e.tensor_mul(t["t1_f"][:], buf, psa[:]))
        p.run("vector", lambda e, dst=dst: e.tensor_add(dst, t["t1_f"][:], psb[:]),
              inc="dve")
        done[k] = p.cnt["dve"]


def emit_l0_attnprep(p, t, din):
    res = t["res_a"]
    consumed = {}
    for k in range(NC32):
        buf = t["w_db"][:, (k % 2) * 4096:(k % 2) * 4096 + 4096]
        if k >= 2:
            p.wait("gpsimd", "dve", consumed[k - 2])
        p.run("gpsimd", lambda e, buf=buf, k=k: e.dma_start(
            out=buf, in_=din["wq_r"][:, k * 4096:(k + 1) * 4096]), inc="wld", n=16)
        p.wait("vector", "wld")
        p.run("vector", lambda e, buf=buf: e.reduce_sum(
            t["wq_red"][:], buf.rearrange("p (d h) -> p d h", h=32), axis=AX.X))
        p.run("vector", lambda e, k=k: e.tensor_copy(
            t["wqsum"][:, k * 128:(k + 1) * 128], t["wq_red"][:]), inc="dve")
        consumed[k] = p.cnt["dve"]
    p.run("sync", lambda e: e.dma_start(out=t["bq32"][:], in_=din["bq"]), inc="ld", n=16)
    p.run("sync", lambda e: e.dma_start(out=t["bkv"][:, 0:1], in_=din["bk"]), inc="ld", n=16)
    p.run("sync", lambda e: e.dma_start(out=t["bkv"][:, 1:2], in_=din["bv"]), inc="ld", n=16)
    p.wait("vector", "ld")
    p.run("vector", lambda e: e.reduce_sum(t["bkv"][:, 2:3], t["bq32"][:], axis=AX.X),
          inc="dve")
    p.run("gpsimd", lambda e: e.dma_start(out=t["wk_sb"][:], in_=din["wk_r"]),
          inc="wld", n=16)
    p.run("gpsimd", lambda e: e.dma_start(out=t["wv_sb"][:], in_=din["wv_r"]),
          inc="wld", n=16)
    psq, psk, psv = t["ps_q"], t["ps_k"], t["ps_v"]
    p.wait("tensor", "dve")
    p.wait("tensor", "wld")
    for k in range(NC32):
        for th in range(2):
            mv = res[:, k * 1024 + th * 512:k * 1024 + th * 512 + 512]
            last = (k == NC32 - 1)
            p.run("tensor", lambda e, k=k, th=th, mv=mv: e.matmul(
                psq[:, th * 512:th * 512 + 512],
                t["wqsum"][:, k * 128:(k + 1) * 128], mv,
                start=(k == 0), stop=(k == NC32 - 1)))
            p.run("tensor", lambda e, k=k, th=th, mv=mv: e.matmul(
                psk[:, th * 512:th * 512 + 512],
                t["wk_sb"][:, k * 128:(k + 1) * 128], mv,
                start=(k == 0), stop=(k == NC32 - 1)))
            p.run("tensor", lambda e, k=k, th=th, mv=mv: e.matmul(
                psv[:, th * 512:th * 512 + 512],
                t["wv_sb"][:, k * 128:(k + 1) * 128], mv,
                start=(k == 0), stop=(k == NC32 - 1)),
                inc=("pe" if last and th == 1 else None))
    p.wait("vector", "pe")
    p.run("vector", lambda e: e.tensor_scalar(t["qsT"][:], psq[:], t["bkv"][:, 2:3],
                                              RS128, ALU.add, ALU.mult))
    p.run("vector", lambda e: e.tensor_scalar(t["kT"][:], psk[:], t["bkv"][:, 0:1],
                                              None, ALU.add))
    p.run("vector", lambda e: e.tensor_scalar(t["vT"][:], psv[:], t["bkv"][:, 1:2],
                                              None, ALU.add), inc="dve")
    p.wait("tensor", "dve")
    for src, dst in [(t["qsT"], t["qs_row"]), (t["kT"], t["k_row"]),
                     (t["vT"], t["v_row"])]:
        for tt in range(8):
            p.run("tensor", lambda e, src=src, tt=tt: e.transpose(
                t["ps_tr"][:], src[:, tt * 128:(tt + 1) * 128], t["ident_f"][:]),
                inc="pe")
            p.wait("vector", "pe")
            p.run("vector", lambda e, dst=dst, tt=tt: e.tensor_copy(
                dst[:, tt * 128:(tt + 1) * 128], t["ps_tr"][:]), inc="dve")
            p.wait("tensor", "dve")


def emit_l0_softmax(p, t):
    expb, dump = t["exp_db"], t["dump_f"]
    dve_done = {}
    for tt in range(8):
        krow = t["k_row"][:, tt * 128:(tt + 1) * 128]
        vrow = t["v_row"][:, tt * 128:(tt + 1) * 128]
        p.wait("scalar", "dve")
        for d in range(128):
            buf = expb[:, (d % 2) * 128:(d % 2) * 128 + 128]
            if d >= 2:
                p.wait("scalar", "dve", dve_done[d - 2])
            p.run("scalar", lambda e, buf=buf, krow=krow, tt=tt, d=d: e.activation(
                buf, krow, AF.Exp,
                scale=t["qs_row"][:, tt * 128 + d:tt * 128 + d + 1],
                accum_out=t["zrow"][:, d:d + 1]), inc="act")
            p.wait("vector", "act")
            p.run("vector", lambda e, buf=buf, vrow=vrow, d=d: e.scalar_tensor_tensor(
                dump[:], buf, 1.0, vrow, ALU.mult, ALU.mult,
                accum_out=t["numrow"][:, d:d + 1]), inc="dve")
            dve_done[d] = p.cnt["dve"]
        p.wait("vector", "act")
        p.run("vector", lambda e: e.reciprocal(t["rz"][:], t["zrow"][:]))
        p.run("vector", lambda e: e.tensor_mul(t["orow"][:], t["numrow"][:], t["rz"][:]),
              inc="dve")
        p.wait("tensor", "dve")
        p.run("tensor", lambda e: e.transpose(t["ps_tr"][:], t["orow"][:],
                                              t["ident_f"][:]), inc="pe")
        p.wait("vector", "pe")
        p.run("vector", lambda e, tt=tt: e.tensor_copy(
            t["oT_bf"][:, tt * 128:(tt + 1) * 128], t["ps_tr"][:]), inc="dve")
        p.wait("tensor", "dve")


def emit_l0_xa(p, t, din):
    res = t["res_a"]
    p.run("gpsimd", lambda e: e.dma_start(out=t["aT_sb"][:], in_=din["aT_r"]),
          inc="wld", n=16)
    p.run("sync", lambda e: e.dma_start(out=t["crep"][:], in_=din["coeffs_rep"]),
          inc="ld", n=16)
    p.run("sync", lambda e: e.dma_start(out=t["srep"][:], in_=din["scales_rep"]),
          inc="ld", n=16)
    p.wait("gpsimd", "ld")
    p.run("gpsimd", lambda e: e.tensor_mul(t["wer"][:], t["crep"][:], t["srep"][:]),
          inc="pool")
    psxa = t["ps_xa"]
    p.wait("tensor", "wld")
    for e2 in range(2):
        for k in range(NC32):
            for th in range(2):
                last = (e2 == 1 and k == NC32 - 1 and th == 1)
                p.run("tensor", lambda e, e2=e2, k=k, th=th: e.matmul(
                    psxa[:, e2 * 1024 + th * 512:e2 * 1024 + th * 512 + 512],
                    t["aT_sb"][:, k * 256 + e2 * 128:k * 256 + e2 * 128 + 128],
                    res[:, k * 1024 + th * 512:k * 1024 + th * 512 + 512],
                    start=(k == 0), stop=(k == NC32 - 1)),
                    inc=("pe" if last else None))
    p.wait("vector", "pe")
    p.wait("vector", "pool")
    for e2 in range(2):
        p.run("vector", lambda e, e2=e2: e.tensor_scalar(
            t["xaT_bf"][:, e2 * 1024:(e2 + 1) * 1024],
            psxa[:, e2 * 1024:(e2 + 1) * 1024],
            t["wer"][:, e2:e2 + 1], None, ALU.mult), inc="dve")


def emit_l0_ffn_update(p, t, din, x_dram, x1_dram):
    res = t["res_a"]
    p.run("gpsimd", lambda e: e.dma_start(out=t["bfT_sb"][:], in_=din["bfT_r"]),
          inc="wld", n=16)
    p.run("sync", lambda e: e.dma_start(out=t["l0bias"][:], in_=din["l0bias_c"]),
          inc="ld", n=16)
    wconsumed = {}
    xconsumed = {}
    evac_done = {}
    for oc in range(NC32):
        psg = t["ps_g2"][oc % 2]
        psob = t["ps_ob2"][oc % 2]
        buf = t["w_db"][:, (oc % 2) * 4096:(oc % 2) * 4096 + 4096]
        if oc >= 2:
            p.wait("gpsimd", "pe", wconsumed[oc - 2])
        p.run("gpsimd", lambda e, buf=buf, oc=oc: e.dma_start(
            out=buf, in_=din["l0w_r"][oc]), inc="wld", n=16)
        xb = t["xc_f"][:, (oc % 2) * 1024:(oc % 2) * 1024 + 1024]
        if oc >= 2:
            p.wait("sync", "dve", xconsumed[oc - 2])
        p.run("sync", lambda e, xb=xb, oc=oc: e.dma_start(
            out=xb, in_=x_dram[oc * 128:(oc + 1) * 128, :]), inc="ld", n=16)
        p.wait("tensor", "wld")
        if oc == 0:
            p.wait("tensor", "dve")
        if oc >= 2:
            p.wait("tensor", "dve", evac_done[oc - 2])
        for th in range(2):
            for k in range(NC32):
                p.run("tensor", lambda e, buf=buf, k=k, th=th, psg=psg: e.matmul(
                    psg[:, th * 512:th * 512 + 512], buf[:, k * 128:(k + 1) * 128],
                    res[:, k * 1024 + th * 512:k * 1024 + th * 512 + 512],
                    start=(k == 0), stop=False))
            for e2 in range(2):
                p.run("tensor", lambda e, oc=oc, e2=e2, th=th, psg=psg: e.matmul(
                    psg[:, th * 512:th * 512 + 512],
                    t["bfT_sb"][:, oc * 256 + e2 * 128:oc * 256 + e2 * 128 + 128],
                    t["xaT_bf"][:, e2 * 1024 + th * 512:e2 * 1024 + th * 512 + 512],
                    start=False, stop=(e2 == 1)))
        p.run("tensor", lambda e, oc=oc, psob=psob: e.matmul(
            psob[:, 0:512], t["sel"][:, oc * 128:(oc + 1) * 128],
            t["oT_bf"][:, 0:512], start=True, stop=True), inc="pe")
        p.run("tensor", lambda e, oc=oc, psob=psob: e.matmul(
            psob[:, 512:1024], t["sel"][:, oc * 128:(oc + 1) * 128],
            t["oT_bf"][:, 512:1024], start=True, stop=True), inc="pe")
        wconsumed[oc] = p.cnt["pe"]
        p.wait("scalar", "pe")
        p.wait("scalar", "ld")
        p.run("scalar", lambda e, oc=oc, psg=psg: e.activation(
            t["g_sb"][:], psg[:], AF.Silu, bias=t["l0bias"][:, oc:oc + 1]), inc="act")
        p.wait("vector", "act")
        p.wait("vector", "pe")
        p.wait("vector", "ld")
        p.wait("vector", "st")
        if "gall" in t:
            p.wait("sync", "act")
            p.wait("sync", "pe")
            p.run("sync", lambda e, oc=oc: e.dma_start(
                out=t["gall"][oc * 128:(oc + 1) * 128, :], in_=t["g_sb"][:]),
                inc="st", n=16)
            p.run("vector", lambda e, psob=psob: e.tensor_copy(t["t2_f"][:], psob[:]), inc="dve")
            p.wait("sync", "dve")
            p.run("sync", lambda e, oc=oc: e.dma_start(
                out=t["oball"][oc * 128:(oc + 1) * 128, :], in_=t["t2_f"][:]),
                inc="st", n=16)
            p.wait("vector", "st")
        p.run("vector", lambda e, psob=psob: e.tensor_mul(t["t1_f"][:], t["g_sb"][:], psob[:]))
        if "uall" in t:
            p.run("vector", lambda e: e.tensor_copy(t["t1_f"][:], t["t1_f"][:]), inc="dve")
            p.wait("sync", "dve")
            p.run("sync", lambda e, oc=oc: e.dma_start(
                out=t["uall"][oc * 128:(oc + 1) * 128, :], in_=t["t1_f"][:]),
                inc="st", n=16)
            p.wait("vector", "st")
        p.run("vector", lambda e, xb=xb: e.tensor_add(t["t2_f"][:], t["t1_f"][:], xb),
              inc="dve")
        xconsumed[oc] = p.cnt["dve"]
        evac_done[oc] = p.cnt["dve"]
        p.wait("sync", "dve")
        p.run("sync", lambda e, oc=oc: e.dma_start(
            out=x1_dram[oc * 128:(oc + 1) * 128, :], in_=t["t2_f"][:]), inc="st", n=16)





def emit_router_xs(p, t, din, xs_dram):
    res = t["res_a"]
    p.run("gpsimd", lambda e: e.dma_start(out=t["gate_sb"][:], in_=din["gate_r"]),
          inc="wld", n=16)
    p.run("gpsimd", lambda e: e.dma_start(out=t["ics_sb"][:], in_=din["ics"]),
          inc="wld", n=16)
    psr, pszb8, psis = t["ps_r8"], t["ps_zb8"], t["ps_is"]
    p.wait("tensor", "wld")
    p.wait("tensor", "pool")
    for k in range(NC32):
        for th in range(2):
            p.run("tensor", lambda e, k=k, th=th: e.matmul(
                psr[:, th * 512:th * 512 + 512],
                t["gate_sb"][:, k * 8:(k + 1) * 8],
                res[:, k * 1024 + th * 512:k * 1024 + th * 512 + 512],
                start=(k == 0), stop=(k == NC32 - 1)),
                inc=("pe" if k == NC32 - 1 and th == 1 else None))
    p.wait("scalar", "pe")
    p.run("scalar", lambda e: e.activation(t["expr8"][:], psr[:], AF.Exp), inc="act")
    p.wait("tensor", "act")
    for th in range(2):
        p.run("tensor", lambda e, th=th: e.matmul(
            t["ps_z8"][:, th * 512:th * 512 + 512], t["ones8_f"][:],
            t["expr8"][:, th * 512:th * 512 + 512], start=True, stop=True),
            inc=("pe" if th == 1 else None))
    p.wait("vector", "pe")
    p.run("vector", lambda e: e.reciprocal(t["rz8"][:], t["ps_z8"][0:1, :]), inc="dve")
    p.wait("tensor", "dve")
    for th in range(2):
        p.run("tensor", lambda e, th=th: e.matmul(
            pszb8[:, th * 512:th * 512 + 512], t["ones1_f"][0:1, 0:8],
            t["rz8"][0:1, th * 512:th * 512 + 512], start=True, stop=True),
            inc=("pe" if th == 1 else None))
    p.wait("vector", "pe")
    p.run("vector", lambda e: e.tensor_mul(t["routT"][:], t["expr8"][:], pszb8[:]),
          inc="dve")
    p.wait("tensor", "dve")
    done = {}
    for k in range(NC32):
        for th in range(2):
            p.run("tensor", lambda e, k=k, th=th: e.matmul(
                psis[:, th * 512:th * 512 + 512],
                t["ics_sb"][:, k * 128:(k + 1) * 128],
                t["routT"][:, th * 512:th * 512 + 512], start=True, stop=True),
                inc=("pe" if th == 1 else None))
        p.wait("vector", "pe")
        if k >= 1:
            p.wait("vector", "st", done[k - 1])
        p.run("vector", lambda e, k=k: e.tensor_mul(
            t["xs_bf"][:], res[:, k * 1024:(k + 1) * 1024], psis[:]), inc="dve")
        p.wait("sync", "dve")
        p.run("sync", lambda e, k=k: e.dma_start(
            out=xs_dram[k * 128:(k + 1) * 128, :], in_=t["xs_bf"][:]), inc="st", n=16)
        done[k] = p.cnt["st"]
        p.wait("tensor", "dve")


def emit_l1_qkv(p, t, din, contrib):
    res = t["res_a"]
    psst = t["ps_st4"]
    wcons = {}
    dcons = {}
    clip_done = {}
    for oc in range(96):
        psqk = t["ps_qk2"][oc % 2]
        buf = t["w_db"][:, (oc % 2) * 4096:(oc % 2) * 4096 + 4096]
        if oc >= 2:
            p.wait("gpsimd", "pe", wcons[oc - 2])
        p.run("gpsimd", lambda e, buf=buf, oc=oc: e.dma_start(
            out=buf, in_=din["wqkv_r"][oc]), inc="wld", n=16)
        p.wait("tensor", "wld")
        if oc >= 2:
            p.wait("tensor", "dve", clip_done[oc - 2])
        for th in range(2):
            for k in range(NC32):
                last = (th == 1 and k == NC32 - 1)
                p.run("tensor", lambda e, buf=buf, k=k, th=th, psqk=psqk: e.matmul(
                    psqk[:, th * 512:th * 512 + 512], buf[:, k * 128:(k + 1) * 128],
                    res[:, k * 1024 + th * 512:k * 1024 + th * 512 + 512],
                    start=(k == 0), stop=(k == NC32 - 1)),
                    inc=("pe" if last else None))
        wcons[oc] = p.cnt["pe"]
        pend = t.get("_qkv_pending", [])
        while pend:
            pend.pop(0)()
        p.wait("vector", "pe")
        if oc >= 2:
            p.wait("vector", "st", dcons[oc - 2])
            p.wait("vector", "pe", t.get("_qkv_stats_pe", 0))
        qk = t["qk_db"][:, (oc % 2) * 1024:(oc % 2) * 1024 + 1024]
        p.run("vector", lambda e, qk=qk, psqk=psqk: e.tensor_scalar(
            qk, psqk[:], -8.0, 8.0, ALU.max, ALU.min), inc="dve")
        clip_done[oc] = p.cnt["dve"]
        if oc < 64:
            p.wait("scalar", "dve")
            p.wait("scalar", "pe", t.get("_qkv_stats_pe", 0))
            sq = t["sq_bf"][:, (oc % 2) * 1024:(oc % 2) * 1024 + 1024]
            p.run("scalar", lambda e, qk=qk, sq=sq: e.activation(sq, qk, AF.Square),
                  inc="act")
            sq_act = p.cnt["act"]
            st0 = (oc % 32 == 0)
            sp0 = (oc % 32 == 31)

            def emit_stats(oc=oc, qk=qk, sq=sq, st0=st0, sp0=sp0, sq_act=sq_act):
                p.wait("tensor", "act", sq_act)
                if st0:
                    p.wait("tensor", "dve")
                for th in range(2):
                    p.run("tensor", lambda e, qk=qk, th=th, st0=st0, sp0=sp0: e.matmul(
                        psst[:, th * 512:th * 512 + 512], t["ones_bf"][:],
                        qk[:, th * 512:th * 512 + 512], start=st0, stop=sp0))
                for th in range(2):
                    p.run("tensor", lambda e, sq=sq, th=th, st0=st0, sp0=sp0: e.matmul(
                        psst[:, 1024 + th * 512:1024 + th * 512 + 512], t["ones_bf"][:],
                        sq[:, th * 512:th * 512 + 512], start=st0, stop=sp0),
                        inc=("pe" if th == 1 else None))
                t["_qkv_stats_pe"] = p.cnt["pe"]
            t.setdefault("_qkv_pending", []).append(emit_stats)
            if sp0:
                base = 0 if oc < 32 else 2

                def emit_finalize(base=base):
                    mu, var = t["r_mu"], t["r_var"]
                    p.wait("vector", "pe")
                    p.run("vector", lambda e: e.tensor_scalar_mul(mu[:], psst[0:1, 0:1024], 1.0 / D))
                    p.run("vector", lambda e: e.tensor_scalar_mul(var[:], psst[0:1, 1024:2048], 1.0 / D))
                    p.run("vector", lambda e: e.tensor_mul(t["r_a"][:], mu[:], mu[:]))
                    p.run("vector", lambda e: e.tensor_sub(var[:], var[:], t["r_a"][:]), inc="dve")
                    p.wait("scalar", "dve")
                    p.run("scalar", lambda e: e.activation(t["r_b"][:], var[:], AF.Ln,
                                                           bias=t["cval"][0:1, 0:1]))
                    p.run("scalar", lambda e, base=base: e.activation(
                        t["ab4"][0:1, base * 1024:base * 1024 + 1024], t["r_b"][:],
                        AF.Exp, scale=t["cval"][0:1, 1:2]), inc="act")
                    p.wait("vector", "act")
                    p.run("vector", lambda e, base=base: e.tensor_mul(
                        t["r_b"][:], mu[:], t["ab4"][0:1, base * 1024:base * 1024 + 1024]))
                    p.run("vector", lambda e, base=base: e.tensor_scalar_mul(
                        t["ab4"][0:1, (base + 1) * 1024:(base + 2) * 1024],
                        t["r_b"][:], -1.0), inc="dve")
                t["_qkv_pending"].append(emit_finalize)
        p.wait("sync", "dve")
        p.run("sync", lambda e, qk=qk, oc=oc: e.dma_start(
            out=contrib[oc * 128:(oc + 1) * 128, :], in_=qk), inc="st", n=16)
        dcons[oc] = p.cnt["st"]


def emit_qk_apply(p, t, din, contrib):
    psq_a, psq_b, psk_a, psk_b = t["ps_aq"], t["ps_bq"], t["ps_ak"], t["ps_bk"]
    p.wait("tensor", "dve")
    for ps, row in ((psq_a, 0), (psq_b, 1), (psk_a, 2), (psk_b, 3)):
        for th in range(2):
            p.run("tensor", lambda e, ps=ps, row=row, th=th: e.matmul(
                ps[:, th * 512:th * 512 + 512], t["ones1_f"][:],
                t["ab4"][0:1, row * 1024 + th * 512:row * 1024 + th * 512 + 512],
                start=True, stop=True), inc=("pe" if th == 1 else None))
    done = {}
    for ch in range(64):
        isq = ch < 32
        psa = psq_a if isq else psk_a
        psb = psq_b if isq else psk_b
        wln = t["qlnw"] if isq else t["klnw"]
        bln = t["qlnb"] if isq else t["klnb"]
        col = ch % 32
        buf = t["qk_db"][:, (ch % 4) * 1024:(ch % 4) * 1024 + 1024]
        if ch >= 4:
            p.wait("sync", "st", done[ch - 4])
        p.run("sync", lambda e, buf=buf, ch=ch: e.dma_start(
            out=buf, in_=contrib[ch * 128:(ch + 1) * 128, :]), inc="ld", n=16)
        p.wait("vector", "ld")
        p.wait("vector", "pe")
        p.run("vector", lambda e, buf=buf, psa=psa, wln=wln, col=col:
              e.scalar_tensor_tensor(t["t1_f"][:], buf, wln[:, col:col + 1], psa[:],
                                     ALU.mult, ALU.mult))
        p.run("vector", lambda e, psb=psb, wln=wln, col=col:
              e.scalar_tensor_tensor(t["t2_f"][:], psb[:], wln[:, col:col + 1],
                                     t["t1_f"][:], ALU.mult, ALU.add))
        if isq:
            p.run("vector", lambda e, buf=buf, bln=bln, col=col: e.tensor_scalar(
                buf, t["t2_f"][:], bln[:, col:col + 1], RS128, ALU.add, ALU.mult),
                inc="dve")
        else:
            p.run("vector", lambda e, buf=buf, bln=bln, col=col: e.tensor_scalar(
                buf, t["t2_f"][:], bln[:, col:col + 1], None, ALU.add), inc="dve")
        p.wait("sync", "dve")
        p.run("sync", lambda e, buf=buf, ch=ch: e.dma_start(
            out=contrib[ch * 128:(ch + 1) * 128, :], in_=buf), inc="st", n=16)
        done[ch] = p.cnt["st"]


def emit_sign_stream(p, t, din, signw_d):
    act_mark = {}
    st_mark = {}
    for i in range(NC32):
        buf = t["sg_raw"][:, (i % 2) * 4096:(i % 2) * 4096 + 4096]
        if i >= 2:
            p.wait("gpsimd", "act", act_mark[i - 2])
        p.run("gpsimd", lambda e, buf=buf, i=i: e.dma_start(
            out=buf, in_=din["signw_r"][i]), inc="wld", n=16)
        p.wait("scalar", "wld")
        if i >= 1:
            p.wait("scalar", "st", st_mark[i - 1])
        p.run("scalar", lambda e, buf=buf: e.activation(t["sg_out"][:], buf, AF.Sign),
              inc="act")
        act_mark[i] = p.cnt["act"]
        p.wait("sync", "act")
        p.run("sync", lambda e, i=i: e.dma_start(
            out=signw_d[i * 128:(i + 1) * 128, :], in_=t["sg_out"][:]), inc="st", n=16)
        st_mark[i] = p.cnt["st"]


def emit_allgather(p, t, in_ap, out_ap, sem="cc", full_join=False):
    if full_join:
        p.join()
    else:
        for sem_ in ("st", "ld", "dve"):
            p.wait("gpsimd", sem_)
    p.run("gpsimd", lambda e: e.collective_compute(
        "AllGather", ALU.bypass,
        replica_groups=[[0, 1, 2, 3], [4, 5, 6, 7]],
        ins=[in_ap.opt()], outs=[out_ap.opt()]), inc=sem, n=1)


def emit_flash(p, t, qkv_all, attn_contrib):
    p.wait("vector", "pool")
    p.wait("tensor", "pool")
    p.run("sync", lambda e: e.dma_start(out=t["slopes1"][:], in_=t["din_slopes"]),
          inc="ld", n=16)
    p.wait("tensor", "ld")
    p.run("tensor", lambda e: e.matmul(t["ps_zb"][:, 0:8], t["ones1_f"][:],
                                       t["slopes1"][0:1, :], start=True, stop=True),
          inc="pe")
    p.wait("scalar", "pe")
    p.run("scalar", lambda e: e.copy(t["slope_b"][:], t["ps_zb"][:, 0:8]),
          inc="act")
    kcons = {}
    for h in range(8):
        if h >= 1:
            p.wait("sync", "pe", kcons[h - 1])

        p.wait("sync", "cc")
        def body(e, j, h=h):
            hg = 8 * j + h
            for rr in range(4):
                e.dma_start(out=t["k_sb"][:, rr * 1024:(rr + 1) * 1024],
                            in_=qkv_all[rr * 12288 + 4096 + hg * 128:
                                        rr * 12288 + 4096 + hg * 128 + 128, :]
                            ).then_inc(p.sem["ld"], 16)
                e.dma_start(out=t["v_stage"][:, rr * 1024:(rr + 1) * 1024],
                            in_=qkv_all[rr * 12288 + 8192 + hg * 128:
                                        rr * 12288 + 8192 + hg * 128 + 128, :]
                            ).then_inc(p.sem["ld"], 16)
            for qc in range(8):
                e.dma_start(out=t["q_sb"][:, qc * 512:(qc + 1) * 512],
                            in_=qkv_all[(qc // 2) * 12288 + hg * 128:
                                        (qc // 2) * 12288 + hg * 128 + 128,
                                        (qc % 2) * 512:(qc % 2) * 512 + 512]
                            ).then_inc(p.sem["ld"], 16)
        p.run_switch("sync", [lambda e, j=j, b=body: b(e, j) for j in range(4)],
                     incs=[("ld", 16 * 16)])
        p.wait("tensor", "ld")
        p.wait("tensor", "dve")
        for tau in range(32):
            p.run("tensor", lambda e, tau=tau: e.transpose(
                t["ps_trv"][:], t["v_stage"][:, tau * 128:(tau + 1) * 128],
                t["ident_bf"][:]), inc="pe")
            p.wait("vector", "pe")
            p.run("vector", lambda e, tau=tau: e.tensor_copy(
                t["v_row"][:, tau * 128:(tau + 1) * 128], t["ps_trv"][:]), inc="dve")
            p.wait("tensor", "dve")
        if h == 0 and "fk" in t:
            p.wait("sync", "dve")
            p.run("sync", lambda e: e.dma_start(out=t["fk"], in_=t["k_sb"][:]), inc="st", n=16)
            p.run("sync", lambda e: e.dma_start(out=t["fq"], in_=t["q_sb"][:]), inc="st", n=16)
            p.run("sync", lambda e: e.dma_start(out=t["fvr"], in_=t["v_row"][:]), inc="st", n=16)
            p.wait("sync", "st")
        p.wait("vector", "act")
        p.run("vector", lambda e, h=h: e.tensor_scalar(
            t["bv_all"][:], t["iota_bv"][:], t["slope_b"][:, h:h + 1],
            -CSHIFT, ALU.mult, ALU.add))
        p.run("vector", lambda e, h=h: e.tensor_scalar(
            t["fvec"][:], t["iota_f"][:], t["slope_b"][0:1, h:h + 1], -1.0,
            ALU.mult, ALU.mult), inc="dve")
        p.wait("scalar", "dve")
        p.wait("tensor", "dve")
        for qc in range(8):
            nkt = 4 * qc + 4
            qpar = (h * 8 + qc) % 2
            pso = t["ps_o2"][qpar]
            psz = t["ps_z2"][qpar]
            s_consumed = {}
            act_of = {}

            def emit_S(kt, qc=qc, nkt=nkt):
                par = kt % 2
                if kt >= 2:
                    sem_, val_ = s_consumed[kt - 2]
                    p.wait("tensor", sem_, val_)
                p.run("tensor", lambda e, qc=qc, kt=kt, par=par: e.matmul(
                    t["ps_S"][:, par * 512:par * 512 + 512],
                    t["k_sb"][:, kt * 128:(kt + 1) * 128],
                    t["q_sb"][:, qc * 512:(qc + 1) * 512], start=True, stop=False))
                p.run("tensor", lambda e, par=par: e.matmul(
                    t["ps_S"][:, par * 512:par * 512 + 512],
                    t["ones1_bf"][:], t["fvec"][:], start=False, stop=True),
                    inc="pe")
                scnt = p.cnt["pe"]
                m = kt - 4 * qc + 32
                if m >= 32:
                    p.wait("vector", "pe", scnt)
                    p.wait("vector", "act")
                    p.run("vector", lambda e, par=par, m=m: e.tensor_add(
                        t["arg_db"][:, par * 512:par * 512 + 512],
                        t["ps_S"][:, par * 512:par * 512 + 512],
                        t["m01"][:, (m - 32) * 512:(m - 31) * 512]), inc="dve")
                    s_consumed[kt] = ("dve", p.cnt["dve"])
                    p.wait("scalar", "dve", p.cnt["dve"])
                    p.run("scalar", lambda e, par=par, m=m: e.activation(
                        t["pt_db"][:, par * 512:par * 512 + 512],
                        t["arg_db"][:, par * 512:par * 512 + 512], AF.Exp,
                        bias=t["bv_all"][:, m:m + 1]), inc="act")
                else:
                    p.wait("scalar", "pe", scnt)
                    p.run("scalar", lambda e, par=par, m=m: e.activation(
                        t["pt_db"][:, par * 512:par * 512 + 512],
                        t["ps_S"][:, par * 512:par * 512 + 512], AF.Exp,
                        bias=t["bv_all"][:, m:m + 1]), inc="act")
                    s_consumed[kt] = ("act", p.cnt["act"])
                act_of[kt] = p.cnt["act"]

            def emit_PV(kt, nkt=nkt, pso=pso, psz=psz):
                par = kt % 2
                p.wait("tensor", "act", act_of[kt])
                p.run("tensor", lambda e, kt=kt, par=par, first=(kt == 0),
                      last=(kt == nkt - 1), pso=pso: e.matmul(
                    pso[:], t["v_row"][:, kt * 128:(kt + 1) * 128],
                    t["pt_db"][:, par * 512:par * 512 + 512],
                    start=first, stop=last))
                p.run("tensor", lambda e, kt=kt, par=par, first=(kt == 0),
                      last=(kt == nkt - 1), psz=psz: e.matmul(
                    psz[:], t["ones_bf"][:],
                    t["pt_db"][:, par * 512:par * 512 + 512],
                    start=first, stop=last), inc="pe")

            p.wait("tensor", "dve", t.setdefault("_fl_o_guard", {}).get(qpar, 0))
            emit_S(0)
            for kt in range(nkt):
                if kt + 1 < nkt:
                    emit_S(kt + 1)
                emit_PV(kt)
            p.wait("vector", "pe")
            p.wait("vector", "st", t.setdefault("_fl_st_guard", {}).get(qpar, 0))
            p.run("vector", lambda e, psz=psz: e.tensor_copy(t["z_s"][:], psz[0:1, :]))
            p.run("vector", lambda e: e.reciprocal(t["rz_fl"][:], t["z_s"][:]),
                  inc="dve")
            p.wait("tensor", "dve")
            p.run("tensor", lambda e: e.matmul(
                t["ps_zb"][:, 0:512], t["ones1_f"][:], t["rz_fl"][:],
                start=True, stop=True), inc="pe")
            p.wait("vector", "pe")
            ab = t["att_bf"][:, qpar * 512:qpar * 512 + 512]
            p.run("vector", lambda e, pso=pso: e.tensor_copy(t["o_f"][:], pso[:]))
            p.run("vector", lambda e, ab=ab: e.tensor_mul(ab, t["o_f"][:],
                                                          t["ps_zb"][:, 0:512]), inc="dve")
            t["_fl_o_guard"][qpar] = p.cnt["dve"]
            p.wait("sync", "dve")
            p.run("sync", lambda e, h=h, qc=qc, ab=ab: e.dma_start(
                out=attn_contrib[h * 128:(h + 1) * 128,
                                 qc * 512:(qc + 1) * 512],
                in_=ab), inc="st", n=16)
            t["_fl_st_guard"][qpar] = p.cnt["st"]
        kcons[h] = p.cnt["pe"]


def emit_l1_out(p, t, din, xs_dram, attn_all, x1_dram, out_dram):
    for k in range(NC32):
        p.run("sync", lambda e, k=k: e.dma_start(
            out=t["res_a"][:, k * 1024:(k + 1) * 1024],
            in_=xs_dram[k * 128:(k + 1) * 128, :]), inc="ld", n=16)

    p.wait("sync", "cc2")
    def body_att(e, j):
        for k in range(NC32):
            e.dma_start(out=t["res_b"][:, k * 1024:(k + 1) * 1024],
                        in_=attn_all[k * 128:(k + 1) * 128,
                                     j * 1024:(j + 1) * 1024]
                        ).then_inc(p.sem["ld"], 16)
    p.run_switch("sync", [lambda e, j=j, b=body_att: b(e, j) for j in range(4)],
                 incs=[("ld", 16 * NC32)])
    if "resb" in t:
        p.wait("sync", "ld")
        p.run("sync", lambda e: e.dma_start(out=t["resb"], in_=t["res_b"][:]),
              inc="st", n=16)
    p.run("gpsimd", lambda e: e.dma_start(out=t["ocs_sb"][:], in_=din["ocs"]),
          inc="wld", n=16)
    p.run("sync", lambda e: e.dma_start(out=t["l1bias"][:], in_=din["l1bias_c"]),
          inc="ld", n=16)
    psA, psB, psos = t["ps_A"], t["ps_B"], t["ps_os"]
    wcons = {}
    xcons = {}
    for oc in range(NC32):
        bufA = t["w_db"][:, (oc % 2) * 4096:(oc % 2) * 4096 + 4096]
        bufB = t["w_db2"][:, (oc % 2) * 4096:(oc % 2) * 4096 + 4096]
        if oc >= 2:
            p.wait("gpsimd", "pe", wcons[oc - 2])
        p.run("gpsimd", lambda e, bufA=bufA, oc=oc: e.dma_start(
            out=bufA, in_=din["wout_r"][oc]), inc="wld", n=16)
        p.run("gpsimd", lambda e, bufB=bufB, oc=oc: e.dma_start(
            out=bufB, in_=t["signw_d"][oc * 128:(oc + 1) * 128, :]), inc="wld", n=16)
        xb = t["xc_f"][:, (oc % 2) * 1024:(oc % 2) * 1024 + 1024]
        if oc >= 2:
            p.wait("sync", "dve", xcons[oc - 2])
        p.run("sync", lambda e, xb=xb, oc=oc: e.dma_start(
            out=xb, in_=x1_dram[oc * 128:(oc + 1) * 128, :]), inc="ld", n=16)
        p.wait("tensor", "wld")
        p.wait("tensor", "ld")
        p.wait("tensor", "act")
        if oc == 0 and "sgb" in t:
            p.wait("sync", "act")
            p.run("sync", lambda e, bufB=bufB: e.dma_start(out=t["sgb"], in_=bufB),
                  inc="st", n=16)
            p.run("sync", lambda e, bufA=bufA: e.dma_start(out=t["sgb2"], in_=bufA),
                  inc="st", n=16)
        if oc == 0:
            p.wait("tensor", "dve")
        else:
            p.wait("tensor", "dve", t.get("_p10_psfree", 0))
        for th in range(2):
            for k in range(NC32):
                p.run("tensor", lambda e, bufA=bufA, k=k, th=th: e.matmul(
                    psA[:, th * 512:th * 512 + 512], bufA[:, k * 128:(k + 1) * 128],
                    t["res_b"][:, k * 1024 + th * 512:k * 1024 + th * 512 + 512],
                    start=(k == 0), stop=(k == NC32 - 1)))
            for k in range(NC32):
                p.run("tensor", lambda e, bufB=bufB, k=k, th=th: e.matmul(
                    psB[:, th * 512:th * 512 + 512], bufB[:, k * 128:(k + 1) * 128],
                    t["res_a"][:, k * 1024 + th * 512:k * 1024 + th * 512 + 512],
                    start=(k == 0), stop=(k == NC32 - 1)))
        for th in range(2):
            p.run("tensor", lambda e, oc=oc, th=th: e.matmul(
                psos[:, th * 512:th * 512 + 512],
                t["ocs_sb"][:, oc * 128:(oc + 1) * 128],
                t["routT"][:, th * 512:th * 512 + 512], start=True, stop=True),
                inc="pe")
        wcons[oc] = p.cnt["pe"]
        p.wait("vector", "pe")
        p.wait("vector", "ld")
        p.wait("vector", "st")
        p.run("vector", lambda e: e.tensor_copy(t["os_f"][:], psos[:]))
        p.run("vector", lambda e: e.tensor_mul(t["t1_f"][:], t["os_f"][:], psB[:]),
              inc="dve")
        if "osall" in t:
            p.wait("sync", "dve")
            p.run("sync", lambda e, oc=oc: e.dma_start(
                out=t["osall"][oc * 128:(oc + 1) * 128, :], in_=t["t1_f"][:]),
                inc="st", n=16)
            p.wait("vector", "st")
        p.wait("scalar", "dve")
        p.run("scalar", lambda e, oc=oc: e.activation(
            t["g_sb"][:], t["t1_f"][:], AF.Silu, bias=t["l1bias"][:, oc:oc + 1]),
            inc="act")
        p.wait("vector", "act")
        if "sall" in t:
            p.run("vector", lambda e: e.tensor_copy(t["t2_f"][:], psB[:]), inc="dve")
            p.wait("sync", "dve")
            p.run("sync", lambda e, oc=oc: e.dma_start(
                out=t["sall"][oc * 128:(oc + 1) * 128, :], in_=t["t2_f"][:]),
                inc="st", n=16)
            p.wait("vector", "st")
        p.run("vector", lambda e: e.tensor_mul(t["t2_f"][:], t["g_sb"][:], psA[:]))
        p.run("vector", lambda e, xb=xb: e.tensor_add(t["t2_f"][:], t["t2_f"][:], xb),
              inc="dve")
        t["_p10_psfree"] = p.cnt["dve"]
        xcons[oc] = p.cnt["dve"]
        p.wait("sync", "dve")
        p.run("sync", lambda e, oc=oc: e.dma_start(
            out=out_dram[oc * 128:(oc + 1) * 128, :], in_=t["t2_f"][:]),
            inc="st", n=16)



def build_program():
    nc = bass.Bass("TRN2", target_bir_lowering=False)
    din = {}

    def ext(name, shape):
        din[name] = nc.dram_tensor(name, shape, F32, kind="ExternalInput").ap()

    ext("x0T", [D, T])
    ext("wq_r", [128, NC32 * 4096])
    ext("bq", [128, 32])
    ext("bk", [128, 1])
    ext("bv", [128, 1])
    ext("wk_r", [128, NC32 * 128])
    ext("wv_r", [128, NC32 * 128])
    l0w = nc.dram_tensor("l0w_r", [NC32, 128, NC32 * 128], F32, kind="ExternalInput").ap()
    din["l0w_r"] = [l0w[i] for i in range(NC32)]
    ext("l0bias_c", [128, 32])
    ext("aT_r", [128, NC32 * 256])
    ext("bfT_r", [128, NC32 * 256])
    ext("coeffs_rep", [128, 2])
    ext("scales_rep", [128, 2])
    wqkv = nc.dram_tensor("wqkv_r", [96, 128, NC32 * 128], F32, kind="ExternalInput").ap()
    din["wqkv_r"] = [wqkv[i] for i in range(96)]
    ext("gate_r", [128, NC32 * 8])
    ext("ics", [8, 4096])
    ext("ocs", [8, 4096])
    ext("qlnw_c", [128, 32])
    ext("qlnb_c", [128, 32])
    ext("klnw_c", [128, 32])
    ext("klnb_c", [128, 32])
    wout = nc.dram_tensor("wout_r", [NC32, 128, NC32 * 128], F32, kind="ExternalInput").ap()
    din["wout_r"] = [wout[i] for i in range(NC32)]
    signw = nc.dram_tensor("signw_r", [NC32, 128, NC32 * 128], F32, kind="ExternalInput").ap()
    din["signw_r"] = [signw[i] for i in range(NC32)]
    ext("l1bias_c", [128, 32])
    ext("slopes", [1, 8])

    x1T = nc.dram_tensor("x1T", [D, T], F32, kind="Internal").ap()
    xsT_d = nc.dram_tensor("xsT_d", [D, T], BF16, kind="Internal").ap()
    qkv_contrib = nc.dram_tensor("qkv_contrib", [3 * D, T], BF16, kind="Internal").ap()
    qkv_all = nc.dram_tensor("qkv_all", [4 * 3 * D, T], BF16, kind="Internal").ap()
    attn_contrib = nc.dram_tensor("attn_contrib", [T, 4096], BF16, kind="Internal").ap()
    attn_all = nc.dram_tensor("attn_all", [D, 4096], BF16, kind="Internal").ap()
    signw_d = nc.dram_tensor("signw_d", [D, D], BF16, kind="Internal").ap()
    out_xT = nc.dram_tensor("out_xT", [D, T], F32, kind="ExternalOutput").ap()
    dbg = {}
    for name in DEBUG_DUMPS:
        shp = {"xn0": [128, NC32 * 1024], "qsT": [128, 1024], "kT": [128, 1024],
               "vT": [128, 1024], "oT": [128, 1024], "wqsum": [128, NC32 * 128],
               "xa": [128, 2048], "gall": [D, T], "oball": [D, T],
               "sel": [128, 32 * 128], "oTd": [128, 1024], "uall": [D, T],
               "xn1": [128, NC32 * 1024], "rout": [8, 1024],
               "xsd": [D, T], "qkvc": [3 * D, T], "attc": [T, 4096],
               "fk": [128, 4096], "fq": [128, 4096], "fvr": [128, 4096],
               "fpt": [128, 512], "fbv": [128, 36],
               "attall": [D, 4096], "resb": [128, NC32 * 1024],
               "osall": [D, T], "sall": [D, T],
               "sgb": [128, 4096], "sgb2": [128, 4096]}[name]
        dbg[name] = nc.dram_tensor("dbg_" + name, shp,
                                   BF16 if name in ("xn0", "oT", "xa", "wqsum", "sel", "oTd", "xn1",
                                                    "rout", "xsd", "qkvc", "attc", "fk", "fq", "fvr", "fpt",
                                                    "attall", "resb", "sgb", "sgb2") else F32,
                                   kind="ExternalOutput").ap()

    p = Prog(nc)
    with ExitStack() as s:
        for sem in ("ld", "wld", "pe", "dve", "act", "pool", "st", "cc", "cc2"):
            p.new_sem(s, sem)
        sb = lambda n, shape, dt: s.enter_context(nc.sbuf_tensor(n, shape, dt)).ap()
        t = {}
        t["ones_bf"] = sb("ones_bf", [128, 1], BF16)
        t["ones1_f"] = sb("ones1_f", [1, 128], F32)
        t["ident_bf"] = sb("ident_bf", [128, 128], BF16)
        t["iota_vec"] = sb("iota_vec", [128, 1], F32)
        t["cval"] = sb("cval", [128, 8], F32)
        t["routT"] = sb("routT", [8, 1024], BF16)
        t["qlnw"] = sb("qlnw", [128, 32], F32)
        t["qlnb"] = sb("qlnb", [128, 32], F32)
        t["klnw"] = sb("klnw", [128, 32], F32)
        t["klnb"] = sb("klnb", [128, 32], F32)
        t["l1bias"] = sb("l1bias", [128, 32], F32)
        t["ones8_f"] = sb("ones8_f", [8, 1], F32)
        t["ones1_bf"] = sb("ones1_bf", [1, 128], BF16)
        t["iota_f"] = sb("iota_f", [1, 512], F32)
        t["res_a"] = sb("res_a", [128, NC32 * 1024], BF16)
        t["sq_bf"] = sb("sq_bf", [128, 2 * 1024], BF16)
        t["xc_f"] = sb("xc_f", [128, 2 * 1024], F32)
        t["t1_f"] = sb("t1_f", [128, 1024], F32)
        t["t2_f"] = sb("t2_f", [128, 1024], F32)
        t["g_sb"] = sb("g_sb", [128, 1024], F32)
        t["w_db"] = sb("w_db", [128, 2 * 4096], BF16)
        t["l0bias"] = sb("l0bias", [128, 32], F32)

        p.init_j("sync", mybir.EngineType.SP)
        emit_consts(p, t)

        with ExitStack() as sln:
            pss = lambda n, shape: sln.enter_context(nc.psum_tensor(n, shape, F32)).ap()
            t["ps_st"] = pss("ps_st0", [1, 2048])
            t["ps_a"] = pss("ps_a0", [128, 1024])
            t["ps_b"] = pss("ps_b0", [128, 1024])
            sbl = lambda n, shape, dt: sln.enter_context(nc.sbuf_tensor(n, shape, dt)).ap()
            t["xb_bf"] = sbl("xb_bf0", [128, 2048], BF16)
            t["r_mu"] = sbl("r_mu0", [1, 1024], F32)
            t["r_var"] = sbl("r_var0", [1, 1024], F32)
            t["r_a"] = sbl("r_a0", [1, 1024], F32)
            t["r_b"] = sbl("r_b0", [1, 1024], F32)
            if "ln" in PHASES:
                emit_ln(p, t, din["x0T"], t["res_a"])
            p.join()
        if "xn0" in dbg:
            p.run("sync", lambda e: e.dma_start(out=dbg["xn0"], in_=t["res_a"][:]),
                  inc="st", n=16)

        with ExitStack() as sl0:
            sb0 = lambda n, shape, dt: sl0.enter_context(nc.sbuf_tensor(n, shape, dt)).ap()
            ps0 = lambda n, shape: sl0.enter_context(nc.psum_tensor(n, shape, F32)).ap()
            t["sel"] = sb0("sel", [128, 32 * 128], BF16)
            t["ident_f"] = sb0("ident_f", [128, 128], F32)
            emit_l0_consts(p, t)
            t["qs_row"] = sb0("qs_row", [128, 1024], F32)
            t["k_row"] = sb0("k_row", [128, 1024], F32)
            t["v_row"] = sb0("v_row", [128, 1024], F32)
            t["oT_bf"] = sb0("oT_bf", [128, 1024], BF16)
            with ExitStack() as s1:
                sb1 = lambda n, shape, dt: s1.enter_context(nc.sbuf_tensor(n, shape, dt)).ap()
                ps1 = lambda n, shape: s1.enter_context(nc.psum_tensor(n, shape, F32)).ap()
                t["wqsum"] = sb1("wqsum", [128, NC32 * 128], BF16)
                t["wq_red"] = sb1("wq_red", [128, 128], F32)
                t["wk_sb"] = sb1("wk_sb", [128, NC32 * 128], BF16)
                t["wv_sb"] = sb1("wv_sb", [128, NC32 * 128], BF16)
                t["bq32"] = sb1("bq32", [128, 32], F32)
                t["bkv"] = sb1("bkv", [128, 3], F32)
                t["qsT"] = sb1("qsT", [128, 1024], F32)
                t["kT"] = sb1("kT", [128, 1024], F32)
                t["vT"] = sb1("vT", [128, 1024], F32)
                t["ps_q"] = ps1("ps_q", [128, 1024])
                t["ps_k"] = ps1("ps_k", [128, 1024])
                t["ps_v"] = ps1("ps_v", [128, 1024])
                t["ps_tr"] = ps1("ps_tr1", [128, 128])
                if "attnprep" in PHASES:
                    emit_l0_attnprep(p, t, din)
                if "wqsum" in dbg:
                    p.wait("sync", "dve")
                    p.run("sync", lambda e: e.dma_start(
                        out=dbg["wqsum"], in_=t["wqsum"][:]), inc="st", n=16)
                for name in ("qsT", "kT", "vT"):
                    if name in dbg:
                        p.wait("sync", "dve")
                        p.run("sync", lambda e, name=name: e.dma_start(
                            out=dbg[name], in_=t[name][:]), inc="st", n=16)
                p.join()
            with ExitStack() as s2:
                sb2 = lambda n, shape, dt: s2.enter_context(nc.sbuf_tensor(n, shape, dt)).ap()
                t["exp_db"] = sb2("exp_db", [128, 2 * 128], F32)
                t["dump_f"] = sb2("dump_f", [128, 128], F32)
                t["zrow"] = sb2("zrow", [128, 128], F32)
                t["numrow"] = sb2("numrow", [128, 128], F32)
                t["rz"] = sb2("rz", [128, 128], F32)
                t["orow"] = sb2("orow", [128, 128], F32)
                t["ps_tr"] = s2.enter_context(nc.psum_tensor("ps_tr2", [128, 128], F32)).ap()
                if "softmax" in PHASES:
                    emit_l0_softmax(p, t)
                p.join()
            if "oT" in dbg:
                p.run("sync", lambda e: e.dma_start(out=dbg["oT"], in_=t["oT_bf"][:]),
                      inc="st", n=16)
            with ExitStack() as s3:
                sb3 = lambda n, shape, dt: s3.enter_context(nc.sbuf_tensor(n, shape, dt)).ap()
                ps3 = lambda n, shape: s3.enter_context(nc.psum_tensor(n, shape, F32)).ap()
                t["aT_sb"] = sb3("aT_sb", [128, NC32 * 256], BF16)
                t["xaT_bf"] = sb3("xaT_bf", [128, 2048], BF16)
                t["crep"] = sb3("crep", [128, 2], F32)
                t["srep"] = sb3("srep", [128, 2], F32)
                t["wer"] = sb3("wer", [128, 2], F32)
                with ExitStack() as s3a:
                    t["ps_xa"] = s3a.enter_context(nc.psum_tensor("ps_xa", [128, 2048], F32)).ap()
                    if "xa" in PHASES:
                        emit_l0_xa(p, t, din)
                    if "xa" in dbg:
                        p.wait("sync", "dve")
                        p.run("sync", lambda e: e.dma_start(
                            out=dbg["xa"], in_=t["xaT_bf"][:]), inc="st", n=16)
                    p.join()
                t["bfT_sb"] = sb3("bfT_sb", [128, NC32 * 256], BF16)
                t["ps_g2"] = [ps3("ps_g_a", [128, 1024]), ps3("ps_g_b", [128, 1024])]
                t["ps_ob2"] = [ps3("ps_ob_a", [128, 1024]), ps3("ps_ob_b", [128, 1024])]
                if "sel" in dbg:
                    p.wait("sync", "pool")
                    p.run("sync", lambda e: e.dma_start(out=dbg["sel"], in_=t["sel"][:]),
                          inc="st", n=16)
                if "oTd" in dbg:
                    p.run("sync", lambda e: e.dma_start(out=dbg["oTd"], in_=t["oT_bf"][:]),
                          inc="st", n=16)
                if "gall" in dbg:
                    t["gall"] = dbg["gall"]
                    t["oball"] = dbg["oball"]
                if "uall" in dbg:
                    t["uall"] = dbg["uall"]
                if "ffn" in PHASES:
                    emit_l0_ffn_update(p, t, din, din["x0T"], x1T)
                p.join()

        p.run("sync", lambda e: e.dma_start(out=t["qlnw"][:], in_=din["qlnw_c"]), inc="ld", n=16)
        p.run("sync", lambda e: e.dma_start(out=t["qlnb"][:], in_=din["qlnb_c"]), inc="ld", n=16)
        p.run("sync", lambda e: e.dma_start(out=t["klnw"][:], in_=din["klnw_c"]), inc="ld", n=16)
        p.run("sync", lambda e: e.dma_start(out=t["klnb"][:], in_=din["klnb_c"]), inc="ld", n=16)
        p.run("sync", lambda e: e.dma_start(out=t["l1bias"][:], in_=din["l1bias_c"]), inc="ld", n=16)
        with ExitStack() as sln1:
            sbl1 = lambda n, shape, dt: sln1.enter_context(nc.sbuf_tensor(n, shape, dt)).ap()
            t["xb_bf"] = sbl1("xb_bf1", [128, 2048], BF16)
            t["r_mu"] = sbl1("r_mu1", [1, 1024], F32)
            t["r_var"] = sbl1("r_var1", [1, 1024], F32)
            t["r_a"] = sbl1("r_a1", [1, 1024], F32)
            t["r_b"] = sbl1("r_b1", [1, 1024], F32)
            t["ab4"] = sbl1("ab4", [1, 4096], F32)
            t["qk_db"] = sbl1("qk_db", [128, 4 * 1024], BF16)
            with ExitStack() as s4:
                pss1 = lambda n, shape: s4.enter_context(nc.psum_tensor(n, shape, F32)).ap()
                t["ps_st"] = pss1("ps_st1", [1, 2048])
                t["ps_a"] = pss1("ps_a1", [128, 1024])
                t["ps_b"] = pss1("ps_b1", [128, 1024])
                emit_ln(p, t, x1T, t["res_a"])
                p.join()
            if "xn1" in dbg:
                p.run("sync", lambda e: e.dma_start(out=dbg["xn1"], in_=t["res_a"][:]),
                      inc="st", n=16)
            with ExitStack() as s5:
                sb5 = lambda n, shape, dt: s5.enter_context(nc.sbuf_tensor(n, shape, dt)).ap()
                ps5 = lambda n, shape: s5.enter_context(nc.psum_tensor(n, shape, F32)).ap()
                t["gate_sb"] = sb5("gate_sb", [128, NC32 * 8], BF16)
                t["ics_sb"] = sb5("ics_sb", [8, 4096], BF16)
                t["expr8"] = sb5("expr8", [8, 1024], F32)
                t["rz8"] = sb5("rz8", [1, 1024], F32)
                t["xs_bf"] = sb5("xs_bf", [128, 1024], BF16)
                t["ps_r8"] = ps5("ps_r8", [8, 1024])
                t["ps_z8"] = ps5("ps_z8", [1, 1024])
                t["ps_zb8"] = ps5("ps_zb8", [8, 1024])
                t["ps_is"] = ps5("ps_is", [128, 1024])
                emit_router_xs(p, t, din, xsT_d)
                p.join()
                if "rout" in dbg:
                    p.run("sync", lambda e: e.dma_start(out=dbg["rout"], in_=t["routT"][:]),
                          inc="st", n=16)
                if "xsd" in dbg:
                    p.run("sync", lambda e: e.dma_start(out=dbg["xsd"], in_=xsT_d),
                          inc="st", n=16)
            with ExitStack() as s6:
                ps6 = lambda n, shape: s6.enter_context(nc.psum_tensor(n, shape, F32)).ap()
                t["ps_qk2"] = [ps6("ps_qk_a", [128, 1024]), ps6("ps_qk_b", [128, 1024])]
                t["ps_st4"] = ps6("ps_st4", [1, 2048])
                emit_l1_qkv(p, t, din, qkv_contrib)
                p.join()
            with ExitStack() as s7:
                ps7 = lambda n, shape: s7.enter_context(nc.psum_tensor(n, shape, F32)).ap()
                t["ps_aq"] = ps7("ps_aq", [128, 1024])
                t["ps_bq"] = ps7("ps_bq", [128, 1024])
                t["ps_ak"] = ps7("ps_ak", [128, 1024])
                t["ps_bk"] = ps7("ps_bk", [128, 1024])
                emit_qk_apply(p, t, din, qkv_contrib)
                with ExitStack() as ssg:
                    t["sg_raw"] = ssg.enter_context(
                        nc.sbuf_tensor("sg_raw", [128, 2 * 4096], BF16)).ap()
                    t["sg_out"] = ssg.enter_context(
                        nc.sbuf_tensor("sg_out", [128, 4096], BF16)).ap()
                    emit_sign_stream(p, t, din, signw_d)
                    p.join()
                if "qkvc" in dbg:
                    p.run("sync", lambda e: e.dma_start(out=dbg["qkvc"], in_=qkv_contrib),
                          inc="st", n=16)
                    p.wait("sync", "st")
        emit_allgather(p, t, qkv_contrib, qkv_all, full_join=True)
        with ExitStack() as s9:
            sb9 = lambda n, shape, dt: s9.enter_context(nc.sbuf_tensor(n, shape, dt)).ap()
            ps9 = lambda n, shape, dt=F32: s9.enter_context(nc.psum_tensor(n, shape, dt)).ap()
            t["m01"] = sb9("m01", [128, 4 * 512], F32)
            t["iota_bv"] = sb9("iota_bv", [128, 36], F32)
            emit_m01(p, t)
            t["k_sb"] = sb9("k_sb", [128, 4096], BF16)
            t["v_stage"] = sb9("v_stage", [128, 4096], BF16)
            t["v_row"] = sb9("v_row_fl", [128, 4096], BF16)
            t["q_sb"] = sb9("q_sb", [128, 4096], BF16)
            t["pt_db"] = sb9("pt_db", [128, 1024], BF16)
            t["arg_db"] = sb9("arg_db", [128, 1024], F32)
            t["fvec"] = sb9("fvec", [1, 512], BF16)
            t["bv_all"] = sb9("bv_all", [128, 36], F32)
            t["slope_b"] = sb9("slope_b", [128, 8], F32)
            t["slopes1"] = sb9("slopes1", [1, 8], F32)
            t["z_s"] = sb9("z_s", [1, 512], F32)
            t["rz_fl"] = sb9("rz_fl", [1, 512], F32)
            t["o_f"] = sb9("o_f", [128, 512], F32)
            t["att_bf"] = sb9("att_bf", [128, 1024], BF16)
            t["ps_S"] = ps9("ps_S", [128, 1024])
            t["ps_o2"] = [ps9("ps_o_a", [128, 512]), ps9("ps_o_b", [128, 512])]
            t["ps_z2"] = [ps9("ps_z_a", [1, 512]), ps9("ps_z_b", [1, 512])]
            t["ps_zb"] = ps9("ps_zb", [128, 512])
            t["ps_trv"] = ps9("ps_trv", [128, 128], BF16)
            t["din_slopes"] = din["slopes"]
            for nm in ("fk", "fq", "fvr", "fpt", "fbv"):
                if nm in dbg:
                    t[nm] = dbg[nm]
            emit_flash(p, t, qkv_all, attn_contrib)
            p.join()
            if "attc" in dbg:
                p.run("sync", lambda e: e.dma_start(out=dbg["attc"], in_=attn_contrib),
                      inc="st", n=16)
                p.wait("sync", "st")
        emit_allgather(p, t, attn_contrib, attn_all, sem="cc2", full_join=True)
        if "attall" in dbg:
            p.wait("sync", "cc2")
            p.run("sync", lambda e: e.dma_start(out=dbg["attall"], in_=attn_all),
                  inc="st", n=16)
        with ExitStack() as s10:
            sb10 = lambda n, shape, dt: s10.enter_context(nc.sbuf_tensor(n, shape, dt)).ap()
            ps10 = lambda n, shape: s10.enter_context(nc.psum_tensor(n, shape, F32)).ap()
            t["res_b"] = sb10("res_b", [128, NC32 * 1024], BF16)
            t["w_db2"] = sb10("w_db2", [128, 2 * 4096], BF16)
            t["ocs_sb"] = sb10("ocs_sb", [8, 4096], BF16)
            t["os_f"] = t["t2_f"]
            if "resb" in dbg:
                t["resb"] = dbg["resb"]
            for nm in ("osall", "sall", "sgb", "sgb2"):
                if nm in dbg:
                    t[nm] = dbg[nm]
            t["ps_A"] = ps10("ps_A", [128, 1024])
            t["ps_B"] = ps10("ps_B", [128, 1024])
            t["ps_os"] = ps10("ps_os", [128, 1024])
            t["signw_d"] = signw_d
            emit_l1_out(p, t, din, xsT_d, attn_all, x1T, out_xT)
            p.join()
        p.wait("sync", "st")

        with nc.Block() as block:
            p.emit(block)
    return nc



def _prep_weights(inputs):
    g = {k: np.asarray(v, dtype=np.float32) for k, v in inputs.items()}
    out = {}
    out["wq_r"] = np.ascontiguousarray(
        g["L0_attn_Wq"].reshape(32, 128, 4096).transpose(1, 0, 2)).reshape(128, -1)
    out["bq"] = np.ascontiguousarray(g["L0_attn_bq"])
    out["bk"] = np.ascontiguousarray(g["L0_attn_bk"])
    out["bv"] = np.ascontiguousarray(g["L0_attn_bv"])
    out["wk_r"] = np.ascontiguousarray(
        g["L0_attn_Wk"][..., 0].reshape(32, 128, 128).transpose(1, 0, 2)).reshape(128, -1)
    out["wv_r"] = np.ascontiguousarray(
        g["L0_attn_Wv"][..., 0].reshape(32, 128, 128).transpose(1, 0, 2)).reshape(128, -1)
    WT = np.ascontiguousarray(g["L0_ffn_W"].T)
    out["l0w_r"] = np.ascontiguousarray(
        WT.reshape(32, 128, 32, 128).transpose(2, 1, 0, 3)).reshape(32, 128, -1)
    out["l0bias_c"] = np.ascontiguousarray(g["L0_ffn_bias"].reshape(32, 128).T)
    AT = np.ascontiguousarray(g["L0_ffn_A"].transpose(2, 0, 1).reshape(4096, 256))
    out["aT_r"] = np.ascontiguousarray(
        AT.reshape(32, 128, 256).transpose(1, 0, 2)).reshape(128, -1)
    BfT = np.ascontiguousarray(g["L0_ffn_Bf"].transpose(0, 2, 1).reshape(256, 4096))
    out["bfT_r"] = np.ascontiguousarray(
        BfT.reshape(2, 128, 32, 128).transpose(1, 2, 0, 3)).reshape(128, -1)
    out["coeffs_rep"] = np.ascontiguousarray(
        np.repeat(g["L0_ffn_coeffs"], 16).reshape(2, 128).T)
    out["scales_rep"] = np.ascontiguousarray(
        np.repeat(g["L0_ffn_scales"], 16).reshape(2, 128).T)
    QT = np.ascontiguousarray(g["L1_attn_w_qkv"].T)
    out["wqkv_r"] = np.ascontiguousarray(
        QT.reshape(32, 128, 96, 128).transpose(2, 1, 0, 3)).reshape(96, 128, -1)
    out["gate_r"] = np.ascontiguousarray(
        g["L1_ffn_gate_w"].T.reshape(32, 128, 8).transpose(1, 0, 2)).reshape(128, -1)
    out["ics"] = np.ascontiguousarray(g["L1_ffn_in_channel_scale"])
    out["ocs"] = np.ascontiguousarray(g["L1_ffn_out_channel_scale"])
    for nm, key in (("qlnw_c", "L1_attn_q_ln_w"), ("qlnb_c", "L1_attn_q_ln_b"),
                    ("klnw_c", "L1_attn_k_ln_w"), ("klnb_c", "L1_attn_k_ln_b")):
        out[nm] = np.ascontiguousarray(g[key].reshape(32, 128).T)
    WO = np.ascontiguousarray(g["L1_attn_w_out"].T)
    out["wout_r"] = np.ascontiguousarray(
        WO.reshape(32, 128, 32, 128).transpose(2, 1, 0, 3)).reshape(32, 128, -1)
    SW = np.ascontiguousarray(g["L1_ffn_weight"].T)
    out["signw_r"] = np.ascontiguousarray(
        SW.reshape(32, 128, 32, 128).transpose(2, 1, 0, 3)).reshape(32, 128, -1)
    out["l1bias_c"] = np.ascontiguousarray(g["L1_ffn_bias"].reshape(32, 128).T)
    return out


def _run(inputs):
    x = np.asarray(inputs["L0_attn_x"], dtype=np.float32)
    B, S, _ = x.shape
    w = _prep_weights(inputs)
    nc = build_program()
    in_maps = []
    for c in range(8):
        b, j = c // 4, c % 4
        xs = np.ascontiguousarray(x[b, j * 1024:(j + 1) * 1024, :].T)
        sl = np.asarray(ALIBI_SLOPES[8 * j:8 * j + 8], np.float32).reshape(1, 8)
        m = {"x0T": xs, "slopes": sl}
        m.update(w)
        in_maps.append(m)
    res = run_bass_kernel_spmd(nc, in_maps, core_ids=list(range(8)))
    out = np.empty((B, S, D), dtype=np.float32)
    for c in range(8):
        b, j = c // 4, c % 4
        out[b, j * 1024:(j + 1) * 1024, :] = res.results[c]["out_xT"].T
    return out, res


def kernel(**inputs):
    out, _ = _run(inputs)
    return out
